# version 01
# baseline (speedup 1.0000x reference)
import numpy as np
from contextlib import ExitStack
import concourse.bass as bass
import concourse.mybir as mybir

F32 = mybir.dt.float32
BF16 = mybir.dt.bfloat16
U8 = mybir.dt.uint8
AF = mybir.ActivationFunctionType
ALU = mybir.AluOpType
AX = mybir.AxisListType

TH = 4096
NT = TH + 128
DM = 1024
DI = 2048
DST = 16
DTR = 64

ENGINES = ("sync", "gpsimd", "vector", "scalar", "tensor")


class Sem:
    def __init__(self, ctx, nc, name):
        self.h = ctx.enter_context(nc.semaphore(name))
        self.n = 0
        self.name = name


class Prog:
    PARANOID = False

    def __init__(self, nc, ctx):
        self.nc = nc
        self.ctx = ctx
        self.ops = {e: [] for e in ENGINES}
        self.sems = {e: Sem(ctx, nc, f"sem_{e}") for e in ENGINES}
        self.dma_sems = {e: Sem(ctx, nc, f"dma_{e}") for e in ("sync", "gpsimd", "scalar")}

    def op(self, eng, fn, waits=(), sem=None, k=1):
        if sem is None:
            sem = self.sems[eng]
        if self.PARANOID:
            waits = list(waits) + [(s, s.n) for s in self.sems.values() if s.n > 0] +                     [(s, s.n) for s in self.dma_sems.values() if s.n > 0]
        sem.n += k
        val = sem.n
        wl = [(s.h, v) for s, v in waits]

        def run(h, fn=fn, wl=wl, semh=sem.h, k=k):
            for hh, vv in wl:
                h.wait_ge(hh, vv)
            fn(h).then_inc(semh, k)

        self.ops[eng].append(run)
        return (sem, val)

    def dma(self, eng, fn, waits=(), depth=1):
        ds = self.dma_sems[eng]
        prev = ds.n - 16 * (depth - 1)
        waits = list(waits) + ([(ds, prev)] if prev > 0 else [])
        return self.op(eng, fn, waits, sem=ds, k=16)

    def raw_wait(self, eng, waits):
        wl = [(s.h, v) for s, v in waits]

        def run(h, wl=wl):
            for hh, vv in wl:
                h.wait_ge(hh, vv)

        self.ops[eng].append(run)

    def end_stage(self, name):
        for e in ("sync", "gpsimd", "scalar"):
            ds = self.dma_sems[e]
            self.raw_wait(e, [(ds, ds.n)])
        with self.nc.Block(name) as blk:
            for e in ENGINES:
                deco = getattr(blk, e)
                ops = self.ops[e]
                if not ops:
                    continue

                @deco
                def _(h, ops=ops):
                    for fn in ops:
                        fn(h)
        self.ops = {e: [] for e in ENGINES}


def chunks_of(total, size):
    out = []
    o = 0
    while o < total:
        c = min(size, total - o)
        out.append((o, c))
        o += c
    return out


def emit_common_consts(p, e, nc):
    c = {}
    c["identbf"] = e(nc.sbuf_tensor("identbf", [128, 128], BF16))
    c["identf"] = e(nc.sbuf_tensor("identf", [128, 128], F32))
    pfi = e(nc.sbuf_tensor("pfi", [128, 128], F32))
    c["epscol"] = e(nc.sbuf_tensor("epscol", [128, 1], F32))
    c["ones1"] = e(nc.sbuf_tensor("ones1", [1, 128], F32))
    c["ones1b"] = e(nc.sbuf_tensor("ones1b", [1, 128], BF16))
    c["ones16"] = e(nc.sbuf_tensor("ones16", [16, 1], F32))
    w_iota = p.op("gpsimd", lambda h: h.iota(pfi[:, :], [[-1, 128]], base=0,
                                             channel_multiplier=1,
                                             allow_small_or_imprecise_dtypes=True))
    p.op("vector", lambda h: h.memset(c["epscol"][:, :], 1e-5))
    p.op("vector", lambda h: h.memset(c["ones1"][:, :], 1.0))
    p.op("vector", lambda h: h.memset(c["ones1b"][:, :], 1.0))
    p.op("vector", lambda h: h.memset(c["ones16"][:, :], 1.0))
    w_id = p.op("vector", lambda h: h.tensor_scalar(c["identbf"][:, :], pfi[:, :],
                                                    0.0, None, ALU.is_equal),
                waits=[w_iota])
    p.op("vector", lambda h: h.tensor_copy(c["identf"][:, :], c["identbf"][:, :]),
         waits=[w_id])
    return c


def emit_xn_stage(p, e, nc, C, xin, xnf, n_tiles, xn_dram_out=None, skip_head=128):
    NTL = n_tiles * 128
    xtm = e(nc.sbuf_tensor("xtm", [128, 2 * DM], F32))
    st6 = e(nc.sbuf_tensor("st6", [128, 12], F32))
    mv = e(nc.sbuf_tensor("mv", [128, 2], F32))
    sd = e(nc.sbuf_tensor("sd", [128, 1], F32))
    rstd = e(nc.sbuf_tensor("rstd", [128, 1], F32))
    xtmn = e(nc.sbuf_tensor("xtmn", [128, 2 * DM], BF16))
    tpps = [e(nc.psum_tensor(f"tpps{j}", [128, 128], BF16)) for j in range(2)]

    tile_done = []
    ev_w = [None, None]
    w_stat_prev = None
    for i in range(n_tiles):
        pb = i % 2
        xv = xtm[:, pb * DM:(pb + 1) * DM]
        xnv = xtmn[:, pb * DM:(pb + 1) * DM]
        waits = [tile_done[i - 2]] if i >= 2 else []
        w_dma = p.dma("sync", lambda h, xv=xv, i=i: h.dma_start(
            out=xv, in_=xin[i * 128:(i + 1) * 128, :]), waits=waits)
        sw = [w_dma] + ([w_stat_prev] if w_stat_prev else [])
        w1 = p.op("vector", lambda h, xv=xv: h.bn_stats(st6[:, 0:6], xv[:, 0:512]),
                  waits=sw)
        w2 = p.op("vector", lambda h, xv=xv: h.bn_stats(st6[:, 6:12], xv[:, 512:1024]),
                  waits=[w_dma])
        w3 = p.op("vector", lambda h: h.bn_aggr(mv[:, :], st6[:, :]), waits=[w1, w2])
        w4 = p.op("scalar", lambda h: h.activation(sd[:, :], mv[:, 1:2], AF.Sqrt,
                                                   bias=C["epscol"][:, :]), waits=[w3])
        w5 = p.op("vector", lambda h: h.reciprocal(rstd[:, :], sd[:, :]), waits=[w4])
        w6 = p.op("vector", lambda h, xv=xv, xnv=xnv: h.tensor_scalar(
            xnv, xv, mv[:, 0:1], rstd[:, :], ALU.subtract, ALU.mult), waits=[w5])
        w_stat_prev = w6
        last = None
        for f in range(8):
            bank = (i * 8 + f) % 2
            tw = [w6] + ([ev_w[bank]] if ev_w[bank] else [])
            w_tp = p.op("tensor", lambda h, bank=bank, f=f, xnv=xnv: h.transpose(
                tpps[bank][:, :], xnv[:, f * 128:(f + 1) * 128], C["identbf"][:, :]),
                waits=tw)
            w_ev = p.op("vector", lambda h, bank=bank, f=f, i=i: h.tensor_copy(
                xnf[:, f * NTL + i * 128: f * NTL + (i + 1) * 128], tpps[bank][:, :]),
                waits=[w_tp])
            ev_w[bank] = w_ev
            last = w_ev
        tile_done.append(last)
    w_all = tile_done[-1]
    if xn_dram_out is not None:
        for f in range(8):
            p.dma("gpsimd", lambda h, f=f: h.dma_start(
                out=xn_dram_out[f * 128:(f + 1) * 128, :],
                in_=xnf[:, f * NTL + skip_head: f * NTL + NTL]), waits=[w_all])
    return w_all


def build_L1():
    nc = bass.Bass("TRN2", target_bir_lowering=False)
    xin = nc.dram_tensor("xin", [NT, DM], F32, kind="ExternalInput").ap()
    wipT = nc.dram_tensor("wipT", [DM, 2 * DI], F32, kind="ExternalInput").ap()
    wcv = nc.dram_tensor("wcv", [128, 64], F32, kind="ExternalInput").ap()
    wcb = nc.dram_tensor("wcb", [128, 16], F32, kind="ExternalInput").ap()
    wxpT = nc.dram_tensor("wxpT", [DI, 96], F32, kind="ExternalInput").ap()
    wdtT = nc.dram_tensor("wdtT", [DTR, DI], F32, kind="ExternalInput").ap()
    dtb = nc.dram_tensor("dtb", [128, 16], F32, kind="ExternalInput").ap()

    xn_o = nc.dram_tensor("xn_o", [DM, TH], BF16, kind="ExternalOutput").ap()
    xc_o = nc.dram_tensor("xc_o", [DI, TH], BF16, kind="ExternalOutput").ap()
    z_o = nc.dram_tensor("z_o", [DI, TH], BF16, kind="ExternalOutput").ap()
    u_o = nc.dram_tensor("u_o", [DI, TH], F32, kind="ExternalOutput").ap()
    k_o = nc.dram_tensor("k_o", [DST, TH], F32, kind="ExternalOutput").ap()
    q_o = nc.dram_tensor("q_o", [DST, TH], F32, kind="ExternalOutput").ap()

    with ExitStack() as octx:
        oe = octx.enter_context
        p = Prog(nc, octx)
        C = emit_common_consts(p, oe, nc)
        xnf = oe(nc.sbuf_tensor("xnf", [128, 8 * NT], BF16))

        with ExitStack() as ctx:
            e = ctx.enter_context
            w_all = emit_xn_stage(p, e, nc, C, xin, xnf, NT // 128,
                                  xn_dram_out=xn_o, skip_head=128)
            p.end_stage("stA")

        with ExitStack() as ctx:
            e = ctx.enter_context
            wip = e(nc.sbuf_tensor("wip", [128, 8 * 2 * DI], BF16))
            wcvs = e(nc.sbuf_tensor("wcvs", [128, 64], F32))
            wcbs = e(nc.sbuf_tensor("wcbs", [128, 16], F32))
            xz = e(nc.sbuf_tensor("xz", [128, NT], F32))
            cacc = e(nc.sbuf_tensor("cacc", [128, TH], F32))
            xcz = e(nc.sbuf_tensor("xcz", [128, NT], BF16))
            pp = [e(nc.psum_tensor(f"pp{j}", [128, 512], F32)) for j in range(2)]

            w_wip = []
            for f in range(8):
                w_wip.append(p.dma("gpsimd", lambda h, f=f: h.dma_start(
                    out=wip[:, f * 2 * DI:(f + 1) * 2 * DI],
                    in_=wipT[f * 128:(f + 1) * 128, :])))
            w_wcv = p.dma("sync", lambda h: h.dma_start(out=wcvs[:, :], in_=wcv[:, :]))
            w_wcb = p.dma("sync", lambda h: h.dma_start(out=wcbs[:, :], in_=wcb[:, :]))

            tcks = chunks_of(NT, 512)
            pev = [None, None]
            w_xz_free = None
            w_cacc_free = None
            w_xcz_free = None
            for fo in range(32):
                is_x = fo < 16
                ev_list = []
                for ci, (c0, cw) in enumerate(tcks):
                    bank = ci % 2
                    mm_w = None
                    for fi in range(8):
                        tw = []
                        if fi == 0:
                            tw = [w_wip[7]]
                            if pev[bank] is not None:
                                tw.append(pev[bank])
                        mm_w = p.op("tensor", lambda h, fi=fi, fo=fo, c0=c0, cw=cw,
                                    bank=bank: h.matmul(
                                        pp[bank][:, 0:cw],
                                        wip[:, fi * 2 * DI + fo * 128:
                                            fi * 2 * DI + (fo + 1) * 128],
                                        xnf[:, fi * NT + c0: fi * NT + c0 + cw],
                                        start=(fi == 0), stop=(fi == 7)), waits=tw)
                    if is_x:
                        evw = [mm_w] + ([w_xz_free] if w_xz_free else [])
                        wv = p.op("vector", lambda h, c0=c0, cw=cw, bank=bank:
                                  h.tensor_copy(xz[:, c0:c0 + cw], pp[bank][:, 0:cw]),
                                  waits=evw)
                    else:
                        evw = [mm_w] + ([w_xcz_free] if w_xcz_free else [])
                        wv = p.op("vector", lambda h, c0=c0, cw=cw, bank=bank:
                                  h.tensor_copy(xcz[:, c0:c0 + cw], pp[bank][:, 0:cw]),
                                  waits=evw)
                    pev[bank] = wv
                    ev_list.append(wv)
                if is_x:
                    cw0 = [ev_list[-1], w_wcv] + ([w_cacc_free] if w_cacc_free else [])
                    w_c = p.op("vector", lambda h, fo=fo: h.tensor_scalar(
                        cacc[:, :], xz[:, 125:125 + TH], wcvs[:, fo * 4:fo * 4 + 1],
                        None, ALU.mult), waits=cw0)
                    for j in (1, 2, 3):
                        w_c = p.op("vector", lambda h, j=j, fo=fo: h.scalar_tensor_tensor(
                            cacc[:, :], xz[:, 125 + j:125 + j + TH],
                            wcvs[:, fo * 4 + j:fo * 4 + j + 1], cacc[:, :],
                            ALU.mult, ALU.add), waits=[w_c])
                    w_xz_free = w_c
                    sw = [w_c, w_wcb] + ([w_xcz_free] if w_xcz_free else [])
                    w_si = p.op("scalar", lambda h, fo=fo: h.activation(
                        xcz[:, 0:TH], cacc[:, :], AF.Silu, bias=wcbs[:, fo:fo + 1]),
                        waits=sw)
                    w_cacc_free = w_si
                    w_xcz_free = p.dma("gpsimd", lambda h, fo=fo: h.dma_start(
                        out=xc_o[fo * 128:(fo + 1) * 128, :], in_=xcz[:, 0:TH]),
                        waits=[w_si])
                else:
                    w_xcz_free = p.dma("gpsimd", lambda h, fo=fo: h.dma_start(
                        out=z_o[(fo - 16) * 128:(fo - 15) * 128, :],
                        in_=xcz[:, 128:NT]), waits=[ev_list[-1]])
            p.end_stage("stB")

        with ExitStack() as ctx:
            e = ctx.enter_context
            wxp = e(nc.sbuf_tensor("wxp", [128, 16 * 96], BF16))
            wdt = e(nc.sbuf_tensor("wdt", [DTR, DI], BF16))
            dtbs = e(nc.sbuf_tensor("dtbs", [128, 16], F32))
            xcc = e(nc.sbuf_tensor("xcc", [128, 2 * 16 * 512], BF16))
            dtl = e(nc.sbuf_tensor("dtl", [DTR, TH], BF16))
            krow = e(nc.sbuf_tensor("krow", [DST, TH], F32))
            qrow = e(nc.sbuf_tensor("qrow", [DST, TH], F32))
            kkrow = e(nc.sbuf_tensor("kkrow", [DST, TH], F32))
            srow = e(nc.sbuf_tensor("srow", [1, TH], F32))
            sbs = e(nc.sbuf_tensor("sbs", [128, 512], F32))
            sgs = e(nc.sbuf_tensor("sgs", [128, 512], F32))
            dens = e(nc.sbuf_tensor("dens", [128, 512], F32))
            recs = e(nc.sbuf_tensor("recs", [128, 512], F32))
            us = e(nc.sbuf_tensor("us", [128, 512], F32))
            pd = e(nc.psum_tensor("pd", [80, 512], F32))
            pq = e(nc.psum_tensor("pq", [16, 512], F32))
            psS = e(nc.psum_tensor("psS", [1, 512], F32))
            pu = e(nc.psum_tensor("pu", [128, 512], F32))
            psb = e(nc.psum_tensor("psb", [128, 512], F32))

            w_wxp = []
            for f in range(16):
                w_wxp.append(p.dma("gpsimd", lambda h, f=f: h.dma_start(
                    out=wxp[:, f * 96:(f + 1) * 96], in_=wxpT[f * 128:(f + 1) * 128, :])))
            w_wdt = p.dma("gpsimd", lambda h: h.dma_start(out=wdt[:, :], in_=wdtT[:, :]))
            w_dtb = p.dma("sync", lambda h: h.dma_start(out=dtbs[:, :], in_=dtb[:, :]))

            xc_r = xc_o.rearrange("(a p) t -> p a t", p=128)
            w_pd_free = None
            w_pq_free = None
            w_psS_free = None
            dtl_ws = []
            for ci in range(8):
                c0 = ci * 512
                pb = ci % 2
                xcv = xcc[:, pb * 16 * 512:(pb + 1) * 16 * 512]
                w_ld = p.dma("sync", lambda h, c0=c0, xcv=xcv: h.dma_start(
                    out=xcv.rearrange("p (a t) -> p a t", a=16),
                    in_=xc_r[:, :, c0:c0 + 512]),
                    waits=[dtl_ws[ci - 2]] if ci >= 2 else [])
                mm1 = mm2 = None
                for fi in range(16):
                    tw = [w_ld, w_wxp[15]]
                    if fi == 0 and w_pd_free is not None:
                        tw.append(w_pd_free)
                    mm1 = p.op("tensor", lambda h, fi=fi, xcv=xcv: h.matmul(
                        pd[:, :], wxp[:, fi * 96:fi * 96 + 80],
                        xcv[:, fi * 512:(fi + 1) * 512],
                        start=(fi == 0), stop=(fi == 15)), waits=tw)
                for fi in range(16):
                    tw = []
                    if fi == 0:
                        tw = [mm1] + ([w_pq_free] if w_pq_free is not None else [])
                    mm2 = p.op("tensor", lambda h, fi=fi, xcv=xcv: h.matmul(
                        pq[:, :], wxp[:, fi * 96 + 80:fi * 96 + 96],
                        xcv[:, fi * 512:(fi + 1) * 512],
                        start=(fi == 0), stop=(fi == 15)), waits=tw)
                w_dtl = p.op("vector", lambda h, c0=c0: h.tensor_copy(
                    dtl[:, c0:c0 + 512], pd[0:64, :]), waits=[mm1])
                w_k = p.op("vector", lambda h, c0=c0: h.tensor_copy(
                    krow[:, c0:c0 + 512], pd[64:80, :]), waits=[mm1])
                w_q = p.op("vector", lambda h, c0=c0: h.tensor_copy(
                    qrow[:, c0:c0 + 512], pq[:, :]), waits=[mm2])
                w_pd_free = w_k
                w_pq_free = w_q
                w_kk = p.op("scalar", lambda h, c0=c0: h.activation(
                    kkrow[:, c0:c0 + 512], krow[:, c0:c0 + 512], AF.Square), waits=[w_k])
                tw = [w_kk]
                if w_psS_free is not None:
                    tw.append(w_psS_free)
                w_s = p.op("tensor", lambda h, c0=c0: h.matmul(
                    psS[:, :], C["ones16"][:, :], kkrow[:, c0:c0 + 512],
                    start=True, stop=True), waits=tw)
                w_psS_free = p.op("vector", lambda h, c0=c0: h.tensor_copy(
                    srow[:, c0:c0 + 512], psS[:, :]), waits=[w_s])
                dtl_ws.append((w_dtl[0], max(w_dtl[1], w_q[1])))
            w_kq1 = p.dma("gpsimd", lambda h: h.dma_start(out=k_o[:, :], in_=krow[:, :]),
                          waits=[w_pd_free])
            w_kq2 = p.dma("gpsimd", lambda h: h.dma_start(out=q_o[:, :], in_=qrow[:, :]),
                          waits=[w_pq_free])

            w_sb_free = None
            w_pu_free = None
            w_chain = None
            for ci in range(8):
                c0 = ci * 512
                tw = [w_psS_free]
                if w_sb_free is not None:
                    tw.append(w_sb_free)
                w_sbm = p.op("tensor", lambda h, c0=c0: h.matmul(
                    psb[:, :], C["ones1"][:, :], srow[:, c0:c0 + 512],
                    start=True, stop=True), waits=tw)
                w_sbc = p.op("vector", lambda h: h.tensor_copy(sbs[:, :], psb[:, :]),
                             waits=[w_sbm] + ([w_chain] if w_chain else []))
                for fo2 in range(16):
                    tw = [dtl_ws[ci], w_wdt]
                    if w_pu_free is not None:
                        tw.append(w_pu_free)
                    w_mm = p.op("tensor", lambda h, fo2=fo2, c0=c0: h.matmul(
                        pu[:, :], wdt[:, fo2 * 128:(fo2 + 1) * 128],
                        dtl[:, c0:c0 + 512], start=True, stop=True), waits=tw)
                    w_sg = p.op("scalar", lambda h, fo2=fo2: h.activation(
                        sgs[:, :], pu[:, :], AF.Sigmoid, bias=dtbs[:, fo2:fo2 + 1]),
                        waits=[w_mm, w_dtb] + ([w_chain] if w_chain else []))
                    w_pu_free = w_sg
                    w_d1 = p.op("vector", lambda h: h.scalar_tensor_tensor(
                        dens[:, :], sgs[:, :], 1.0, sbs[:, :], ALU.mult, ALU.mult),
                        waits=[w_sg, w_sbc])
                    w_d2 = p.op("vector", lambda h: h.tensor_scalar(
                        dens[:, :], dens[:, :], 1.0, None, ALU.add), waits=[w_d1])
                    w_rc = p.op("vector", lambda h: h.reciprocal(recs[:, :], dens[:, :]),
                                waits=[w_d2])
                    w_u = p.op("vector", lambda h: h.tensor_tensor(
                        us[:, :], sgs[:, :], recs[:, :], ALU.mult), waits=[w_rc])
                    w_chain = p.dma("gpsimd", lambda h, fo2=fo2, c0=c0: h.dma_start(
                        out=u_o[fo2 * 128:(fo2 + 1) * 128, c0:c0 + 512], in_=us[:, :]),
                        waits=[w_u])
                w_sb_free = w_chain
            p.end_stage("stC")
    return nc


def l1_in_maps(ins):
    x = ins["L0_attn_hidden_states"]
    xpad = np.concatenate([np.zeros((4, 128, DM), np.float32), x], axis=1)
    wipT = np.ascontiguousarray(ins["L0_attn_in_proj_w"].T)
    wcv = np.ascontiguousarray(
        ins["L0_attn_conv_w"].reshape(DI, 4).reshape(16, 128, 4).transpose(1, 0, 2)
        .reshape(128, 64))
    wcb = np.ascontiguousarray(
        ins["L0_attn_conv_b"].reshape(16, 128).T)
    wxpT = np.ascontiguousarray(ins["L0_attn_x_proj_w"].T)
    wdtT = np.ascontiguousarray(ins["L0_attn_dt_head_w"].T)
    dtb = np.ascontiguousarray(ins["L0_attn_dt_head_b"].reshape(16, 128).T)
    maps = []
    for c in range(8):
        b, hh = c // 2, c % 2
        t0 = hh * TH
        maps.append({
            "xin": np.ascontiguousarray(xpad[b, t0:t0 + NT]),
            "wipT": wipT, "wcv": wcv, "wcb": wcb, "wxpT": wxpT,
            "wdtT": wdtT, "dtb": dtb,
        })
    return maps


def build_L2():
    nc = bass.Bass("TRN2", target_bir_lowering=False)
    xnin = nc.dram_tensor("xnin", [DM, TH], BF16, kind="ExternalInput").ap()
    k0 = nc.dram_tensor("k0", [2 * DM, 4096], U8, kind="ExternalInput").ap()
    k1 = nc.dram_tensor("k1", [4096, 4096], U8, kind="ExternalInput").ap()
    k2 = nc.dram_tensor("k2", [4096, DM], U8, kind="ExternalInput").ap()
    s0c = nc.dram_tensor("s0c", [128, 1], F32, kind="ExternalInput").ap()
    s1c = nc.dram_tensor("s1c", [128, 1], F32, kind="ExternalInput").ap()
    s2c = nc.dram_tensor("s2c", [128, 1], F32, kind="ExternalInput").ap()
    g_o = nc.dram_tensor("g_o", [DM, TH], BF16, kind="ExternalOutput").ap()
    ra_d = nc.dram_tensor("ra_d", [4096, TH], BF16)
    rb_d = nc.dram_tensor("rb_d", [4096, TH], BF16)

    with ExitStack() as octx:
        oe = octx.enter_context
        p = Prog(nc, octx)
        scol = oe(nc.sbuf_tensor("scol", [128, 3], F32))
        w_s = []
        for j, src in enumerate((s0c, s1c, s2c)):
            w_l = p.dma("sync", lambda h, j=j, src=src: h.dma_start(
                out=scol[:, j:j + 1], in_=src[:, :]))
            w_s.append(w_l)
        sc2 = oe(nc.sbuf_tensor("sc2", [128, 3], F32))
        w_sc2 = p.op("vector", lambda h: h.tensor_scalar(
            sc2[:, :], scol[:, :], 2.0, None, ALU.mult), waits=[w_s[-1]])

        with ExitStack() as ctx:
            e = ctx.enter_context
            xns = e(nc.sbuf_tensor("xns", [128, 8 * TH], BF16))
            k0e = e(nc.sbuf_tensor("k0e", [128, 8 * 4096], BF16))
            k0u = e(nc.sbuf_tensor("k0u", [128, 2 * 4096], U8))
            rt = e(nc.sbuf_tensor("rt", [128, 2 * 512], BF16))
            pp = [e(nc.psum_tensor(f"pp{j}", [128, 512], F32)) for j in range(2)]
            w_xn = []
            for f in range(8):
                w_xn.append(p.dma("sync", lambda h, f=f: h.dma_start(
                    out=xns[:, f * TH:(f + 1) * TH], in_=xnin[f * 128:(f + 1) * 128, :])))
            w_k0 = []
            for f in range(8):
                wt = p.dma("gpsimd", lambda h, f=f: h.dma_start(
                    out=k0u[:, 0:4096], in_=k0[f * 128:(f + 1) * 128, :]),
                    waits=[w_k0[-1]] if w_k0 else [])
                wb = p.dma("gpsimd", lambda h, f=f: h.dma_start(
                    out=k0u[:, 4096:8192], in_=k0[DM + f * 128:DM + (f + 1) * 128, :]),
                    waits=[w_k0[-1]] if w_k0 else [])
                w_k0.append(p.op("vector", lambda h, f=f: h.tensor_tensor(
                    k0e[:, f * 4096:(f + 1) * 4096], k0u[:, 0:4096], k0u[:, 4096:8192],
                    ALU.subtract), waits=[wt, wb]))
            pev = [None, None]
            w_rt_free = [None, None]
            for fo in range(32):
                for ci in range(8):
                    c0 = ci * 512
                    bank = (fo * 8 + ci) % 2
                    mm = None
                    for fi in range(8):
                        tw = []
                        if fi == 0:
                            tw = [w_k0[7], w_xn[7]]
                            if pev[bank] is not None:
                                tw.append(pev[bank])
                        mm = p.op("tensor", lambda h, fi=fi, fo=fo, c0=c0, bank=bank:
                                  h.matmul(pp[bank][:, :],
                                           k0e[:, fi * 4096 + fo * 128:
                                               fi * 4096 + (fo + 1) * 128],
                                           xns[:, fi * TH + c0: fi * TH + c0 + 512],
                                           start=(fi == 0), stop=(fi == 7)), waits=tw)
                    rv = rt[:, bank * 512:(bank + 1) * 512]
                    aw = [mm, w_sc2] + ([w_rt_free[bank]] if w_rt_free[bank] else [])
                    w_act = p.op("scalar", lambda h, bank=bank, rv=rv: h.activation(
                        rv, pp[bank][:, :], AF.Relu, scale=sc2[:, 0:1]), waits=aw)
                    pev[bank] = w_act
                    w_rt_free[bank] = p.dma("gpsimd", lambda h, fo=fo, c0=c0, rv=rv:
                                            h.dma_start(out=ra_d[fo * 128:(fo + 1) * 128,
                                                                 c0:c0 + 512], in_=rv),
                                            waits=[w_act])
            p.end_stage("L2D1")

        def dlayer(stname, src_d, wsrc_rows, kmat, kfi_tiles, fo_blocks, scale_idx,
                   actf, dst, dst_rows):
            with ExitStack() as ctx:
                e = ctx.enter_context
                ks = e(nc.sbuf_tensor(f"ks{stname}", [128, kfi_tiles * fo_blocks * 128], BF16))
                ku = e(nc.sbuf_tensor(f"ku{stname}", [128, fo_blocks * 128], U8))
                mt = e(nc.sbuf_tensor(f"mt{stname}", [128, 2 * kfi_tiles * 512], BF16))
                rt = e(nc.sbuf_tensor(f"rt{stname}", [128, 2 * 512], BF16))
                pp = [e(nc.psum_tensor(f"pp{stname}{j}", [128, 512], F32)) for j in range(2)]
                FO = fo_blocks * 128
                w_k = None
                for f in range(kfi_tiles):
                    wu = p.dma("gpsimd", lambda h, f=f: h.dma_start(
                        out=ku[:, :], in_=kmat[f * 128:(f + 1) * 128, 0:FO]),
                        waits=[w_k] if w_k else [])
                    w_k = p.op("vector", lambda h, f=f: h.tensor_scalar(
                        ks[:, f * FO:(f + 1) * FO], ku[:, :], 2.0, -1.0,
                        ALU.mult, ALU.add), waits=[wu])
                src_r = src_d.rearrange("(a p) t -> p a t", p=128)
                pev = [None, None]
                w_rt_free = [None, None]
                w_mt_free = [None, None]
                for ci in range(8):
                    c0 = ci * 512
                    mb = ci % 2
                    mv = mt[:, mb * kfi_tiles * 512:(mb + 1) * kfi_tiles * 512]
                    lw = [wsrc_rows] + ([w_mt_free[mb]] if w_mt_free[mb] else [])
                    w_ld = p.dma("sync", lambda h, c0=c0, mv=mv: h.dma_start(
                        out=mv.rearrange("p (a t) -> p a t", a=kfi_tiles),
                        in_=src_r[:, :, c0:c0 + 512]), waits=lw)
                    last_mm_this_chunk = None
                    for fob in range(fo_blocks):
                        bank = (ci * fo_blocks + fob) % 2
                        mm = None
                        for fi in range(kfi_tiles):
                            tw = []
                            if fi == 0:
                                tw = [w_k, w_ld]
                                if pev[bank] is not None:
                                    tw.append(pev[bank])
                            mm = p.op("tensor", lambda h, fi=fi, fob=fob, mv=mv, bank=bank:
                                      h.matmul(pp[bank][:, :],
                                               ks[:, fi * FO + fob * 128:
                                                   fi * FO + (fob + 1) * 128],
                                               mv[:, fi * 512:(fi + 1) * 512],
                                               start=(fi == 0), stop=(fi == kfi_tiles - 1)),
                                      waits=tw)
                        rv = rt[:, bank * 512:(bank + 1) * 512]
                        aw = [mm, w_sc2] + ([w_rt_free[bank]] if w_rt_free[bank] else [])
                        w_act = p.op("scalar", lambda h, bank=bank, rv=rv: h.activation(
                            rv, pp[bank][:, :], actf, scale=sc2[:, scale_idx:scale_idx + 1]),
                            waits=aw)
                        pev[bank] = w_act
                        w_rt_free[bank] = p.dma("gpsimd", lambda h, fob=fob, c0=c0, rv=rv:
                                                h.dma_start(out=dst[fob * 128:(fob + 1) * 128,
                                                                    c0:c0 + 512], in_=rv),
                                                waits=[w_act])
                        last_mm_this_chunk = mm
                    w_mt_free[mb] = last_mm_this_chunk
                p.end_stage(stname)
                return w_rt_free

        for fog in range(4):
            dlayer(f"L2D2g{fog}", ra_d, (p.dma_sems["gpsimd"], p.dma_sems["gpsimd"].n),
                   k1.rearrange("a b -> a b")[:, fog * 1024:(fog + 1) * 1024]
                   if False else k1[:, fog * 1024:(fog + 1) * 1024],
                   32, 8, 1, AF.Relu,
                   rb_d[fog * 1024:(fog + 1) * 1024, :], 1024)
        dlayer("L2D3", rb_d, (p.dma_sems["gpsimd"], p.dma_sems["gpsimd"].n),
               k2, 32, 8, 2, AF.Silu, g_o, 1024)
    return nc


def l2_in_maps(ins, l1_results):
    k0 = np.ascontiguousarray(ins["L0_ffn_k0"]).view(np.uint8)
    k1 = np.ascontiguousarray(ins["L0_ffn_k1"]).view(np.uint8)
    k2 = np.ascontiguousarray(ins["L0_ffn_k2"]).view(np.uint8)
    maps = []
    for c in range(8):
        maps.append({
            "xnin": l1_results[c]["xn_o"],
            "k0": k0, "k1": k1, "k2": k2,
            "s0c": np.full((128, 1), ins["L0_ffn_s0"], np.float32),
            "s1c": np.full((128, 1), ins["L0_ffn_s1"], np.float32),
            "s2c": np.full((128, 1), ins["L0_ffn_s2"], np.float32),
        })
    return maps


def build_L3(L=8192, TCW=1024, ND=8):
    DMX = ND * 128
    NCH = L // TCW
    nc = bass.Bass("TRN2", target_bir_lowering=False)
    u_sc = nc.dram_tensor("u_sc", [DMX, L], F32, kind="ExternalInput").ap()
    xc_sc = nc.dram_tensor("xc_sc", [DMX, L], BF16, kind="ExternalInput").ap()
    z_sc = nc.dram_tensor("z_sc", [DMX, L], BF16, kind="ExternalInput").ap()
    k_sc = nc.dram_tensor("k_sc", [DST, L], F32, kind="ExternalInput").ap()
    q_sc = nc.dram_tensor("q_sc", [DST, L], F32, kind="ExternalInput").ap()
    Dcol = nc.dram_tensor("Dcol", [128, ND], F32, kind="ExternalInput").ap()
    yg_o = nc.dram_tensor("yg_o", [DMX, L], BF16, kind="ExternalOutput").ap()

    u_r = u_sc.rearrange("(a p) t -> p a t", p=128)
    xc_r = xc_sc.rearrange("(a p) t -> p a t", p=128)
    z_r = z_sc.rearrange("(a p) t -> p a t", p=128)

    with ExitStack() as octx:
        e = octx.enter_context
        p = Prog(nc, octx)
        C = emit_common_consts(p, e, nc)
        ones1 = C["ones1"]

        kn0 = e(nc.sbuf_tensor("kn0", [1, TCW], F32))
        qn0 = e(nc.sbuf_tensor("qn0", [1, TCW], F32))
        kkn0 = e(nc.sbuf_tensor("kkn0", [1, TCW], F32))
        Dc = e(nc.sbuf_tensor("Dc", [128, ND], F32))
        carr = e(nc.sbuf_tensor("carr", [128, 128], F32))
        ucn = e(nc.sbuf_tensor("ucn", [128, ND * TCW], F32))
        xccn = e(nc.sbuf_tensor("xccn", [128, ND * TCW], BF16))
        zcn = e(nc.sbuf_tensor("zcn", [128, ND * TCW], BF16))
        xucn = e(nc.sbuf_tensor("xucn", [128, ND * TCW], F32))
        yacc = e(nc.sbuf_tensor("yacc", [128, ND * TCW], F32))
        kkb = e(nc.sbuf_tensor("kkb", [128, TCW], F32))
        kb = e(nc.sbuf_tensor("kb", [128, TCW], F32))
        qb = e(nc.sbuf_tensor("qb", [128, TCW], F32))
        am = e(nc.sbuf_tensor("am", [128, TCW], F32))
        bm = e(nc.sbuf_tensor("bm", [128, TCW], F32))
        sm = e(nc.sbuf_tensor("sm", [128, TCW], F32))
        gm = e(nc.sbuf_tensor("gm", [128, TCW], F32))
        szm = e(nc.sbuf_tensor("szm", [128, TCW], F32))
        ygm = e(nc.sbuf_tensor("ygm", [128, TCW], BF16))
        pb = [e(nc.psum_tensor(f"pb{j}", [128, 512], F32)) for j in range(2)]

        w_D = p.dma("sync", lambda h: h.dma_start(out=Dc[:, :], in_=Dcol[:, :]))
        w_carr = p.op("vector", lambda h: h.memset(carr[:, :], 0.0))

        w_u_free = None
        w_xc_free = None
        w_z_free = None
        w_xu_free = None
        w_yacc_free = None
        pev = [None, None]
        w_scan_prev = w_carr
        for ci in range(NCH):
            c0 = ci * TCW
            w_u = p.dma("sync", lambda h, c0=c0: h.dma_start(
                out=ucn[:, :].rearrange("p (a t) -> p a t", a=ND),
                in_=u_r[:, :, c0:c0 + TCW]), waits=[w_u_free] if w_u_free else [])
            w_xc = p.dma("sync", lambda h, c0=c0: h.dma_start(
                out=xccn[:, :].rearrange("p (a t) -> p a t", a=ND),
                in_=xc_r[:, :, c0:c0 + TCW]), waits=[w_xc_free] if w_xc_free else [])
            w_z = p.dma("gpsimd", lambda h, c0=c0: h.dma_start(
                out=zcn[:, :].rearrange("p (a t) -> p a t", a=ND),
                in_=z_r[:, :, c0:c0 + TCW]), waits=[w_z_free] if w_z_free else [])
            w_xu = []
            for dt_ in range(ND):
                tw = [w_u, w_xc] + ([w_xu_free] if (dt_ == 0 and w_xu_free) else [])
                w_xu.append(p.op("vector", lambda h, dt_=dt_: h.tensor_tensor(
                    xucn[:, dt_ * TCW:(dt_ + 1) * TCW],
                    xccn[:, dt_ * TCW:(dt_ + 1) * TCW],
                    ucn[:, dt_ * TCW:(dt_ + 1) * TCW], ALU.mult), waits=tw))
            w_rowfree = None
            for n in range(16):
                lw = [w_rowfree] if w_rowfree else []
                w_lk = p.dma("sync", lambda h, n=n, c0=c0: h.dma_start(
                    out=kn0[:, :], in_=k_sc[n:n + 1, c0:c0 + TCW]), waits=lw)
                w_lq = p.dma("sync", lambda h, n=n, c0=c0: h.dma_start(
                    out=qn0[:, :], in_=q_sc[n:n + 1, c0:c0 + TCW]), waits=lw)
                w_kk = p.op("scalar", lambda h: h.activation(
                    kkn0[:, :], kn0[:, :], AF.Square), waits=[w_lk] + lw)
                w_cp = []
                NQ = TCW // 512
                for j, (row, rw, dstt) in enumerate(
                        ((kkn0, w_kk, kkb), (kn0, w_lk, kb), (qn0, w_lq, qb))):
                    for qt in range(NQ):
                        bank = (n * 3 * NQ + j * NQ + qt) % 2
                        tw = [rw]
                        if pev[bank] is not None:
                            tw.append(pev[bank])
                        w_mm = p.op("tensor", lambda h, row=row, qt=qt, bank=bank:
                                    h.matmul(pb[bank][:, :], ones1[:, :],
                                             row[:, qt * 512:(qt + 1) * 512],
                                             start=True, stop=True), waits=tw)
                        wcp = p.op("vector", lambda h, dstt=dstt, qt=qt, bank=bank:
                                   h.tensor_copy(dstt[:, qt * 512:(qt + 1) * 512],
                                                 pb[bank][:, :]), waits=[w_mm])
                        pev[bank] = wcp
                        w_cp.append(wcp)
                for dt_ in range(ND):
                    idx = n * ND + dt_
                    uv = ucn[:, dt_ * TCW:(dt_ + 1) * TCW]
                    xuv = xucn[:, dt_ * TCW:(dt_ + 1) * TCW]
                    yv = yacc[:, dt_ * TCW:(dt_ + 1) * TCW]
                    w_a1 = p.op("vector", lambda h, uv=uv: h.scalar_tensor_tensor(
                        am[:, :], uv, 1.0, kkb[:, :], ALU.mult, ALU.mult),
                        waits=[w_u, w_cp[1 * (TCW // 512) - 1]])
                    w_a2 = p.op("vector", lambda h: h.tensor_scalar(
                        am[:, :], am[:, :], -1.0, 1.0, ALU.mult, ALU.add), waits=[w_a1])
                    w_b = p.op("vector", lambda h, xuv=xuv: h.tensor_tensor(
                        bm[:, :], xuv, kb[:, :], ALU.mult), waits=[w_xu[dt_], w_cp[2 * (TCW // 512) - 1]])
                    w_sc = p.op("vector", lambda h, idx=idx: h.tensor_tensor_scan(
                        sm[:, :], am[:, :], bm[:, :], carr[:, idx:idx + 1],
                        ALU.mult, ALU.add), waits=[w_a2, w_b, w_scan_prev])
                    w_cr = p.op("vector", lambda h, idx=idx: h.tensor_copy(
                        carr[:, idx:idx + 1], sm[:, TCW - 1:TCW]), waits=[w_sc])
                    if n == 0:
                        w_y = p.op("vector", lambda h, yv=yv: h.scalar_tensor_tensor(
                            yv, sm[:, :], 1.0, qb[:, :], ALU.mult, ALU.mult),
                            waits=[w_sc, w_cp[-1]] +
                                  ([w_yacc_free] if (dt_ == 0 and ci == 0 and False) else []))
                    else:
                        w_y = p.op("vector", lambda h, yv=yv: h.scalar_tensor_tensor(
                            sm[:, :], sm[:, :], 1.0, qb[:, :], ALU.mult, ALU.mult),
                            waits=[w_cr, w_cp[-1]])
                        w_y = p.op("vector", lambda h, yv=yv: h.tensor_tensor(
                            yv, yv, sm[:, :], ALU.add), waits=[w_y])
                    w_scan_prev = w_y
                w_rowfree = w_cp[-1]
            w_g_last = None
            for dt_ in range(ND):
                yv = yacc[:, dt_ * TCW:(dt_ + 1) * TCW]
                xcv = xccn[:, dt_ * TCW:(dt_ + 1) * TCW]
                zv = zcn[:, dt_ * TCW:(dt_ + 1) * TCW]
                w_g1 = p.op("vector", lambda h, xcv=xcv, dt_=dt_: h.tensor_scalar(
                    gm[:, :], xcv, Dc[:, dt_:dt_ + 1], None, ALU.mult),
                    waits=[w_scan_prev, w_D] + ([w_g_last] if w_g_last else []))
                w_g2 = p.op("vector", lambda h, yv=yv: h.tensor_tensor(
                    gm[:, :], gm[:, :], yv, ALU.add), waits=[w_g1])
                w_sz = p.op("scalar", lambda h, zv=zv: h.activation(
                    szm[:, :], zv, AF.Silu), waits=[w_z, w_g1])
                w_g3 = p.op("vector", lambda h: h.tensor_tensor(
                    ygm[:, :], gm[:, :], szm[:, :], ALU.mult), waits=[w_g2, w_sz])
                w_g_last = p.dma("gpsimd", lambda h, dt_=dt_, c0=c0: h.dma_start(
                    out=yg_o[dt_ * 128:(dt_ + 1) * 128, c0:c0 + TCW],
                    in_=ygm[:, :]), waits=[w_g3])
            w_u_free = w_g_last
            w_xc_free = w_g_last
            w_z_free = w_g_last
            w_xu_free = w_g_last
            w_yacc_free = w_g_last
        p.end_stage("L3")
    return nc


def l3_in_maps(ins, l1r):
    maps = []
    for c in range(8):
        b, dh = c // 2, c % 2
        u_full = np.concatenate([l1r[2 * b]["u_o"], l1r[2 * b + 1]["u_o"]], axis=1)
        xc_full = np.concatenate([l1r[2 * b]["xc_o"], l1r[2 * b + 1]["xc_o"]], axis=1)
        z_full = np.concatenate([l1r[2 * b]["z_o"], l1r[2 * b + 1]["z_o"]], axis=1)
        k_full = np.concatenate([l1r[2 * b]["k_o"], l1r[2 * b + 1]["k_o"]], axis=1)
        q_full = np.concatenate([l1r[2 * b]["q_o"], l1r[2 * b + 1]["q_o"]], axis=1)
        r0 = dh * DM
        maps.append({
            "u_sc": np.ascontiguousarray(u_full[r0:r0 + DM]),
            "xc_sc": np.ascontiguousarray(xc_full[r0:r0 + DM]),
            "z_sc": np.ascontiguousarray(z_full[r0:r0 + DM]),
            "k_sc": np.ascontiguousarray(k_full),
            "q_sc": np.ascontiguousarray(q_full),
            "Dcol": np.ascontiguousarray(
                ins["L0_attn_D"][r0:r0 + DM].reshape(8, 128).T),
        })
    return maps


def build_L4():
    nc = bass.Bass("TRN2", target_bir_lowering=False)
    ygf = nc.dram_tensor("ygf", [DI, TH], BF16, kind="ExternalInput").ap()
    wopT = nc.dram_tensor("wopT", [DI, DM], F32, kind="ExternalInput").ap()
    x_tm = nc.dram_tensor("x_tm", [TH, DM], F32, kind="ExternalInput").ap()
    g0 = nc.dram_tensor("g0", [DM, TH], BF16, kind="ExternalInput").ap()
    wvT = nc.dram_tensor("wvT", [DM, DM], F32, kind="ExternalInput").ap()
    x1_o = nc.dram_tensor("x1_o", [DM, TH], F32, kind="ExternalOutput").ap()
    xn1_o = nc.dram_tensor("xn1_o", [DM, TH], BF16, kind="ExternalOutput").ap()
    v_o = nc.dram_tensor("v_o", [TH, DM], BF16, kind="ExternalOutput").ap()

    yg_r = ygf.rearrange("(a p) t -> p a t", p=128)

    with ExitStack() as octx:
        e = octx.enter_context
        p = Prog(nc, octx)
        C = emit_common_consts(p, e, nc)
        identf, identbf = C["identf"], C["identbf"]
        onesc = e(nc.sbuf_tensor("onesc", [128, 1], F32))
        p.op("vector", lambda h: h.memset(onesc[:, :], 1.0))

        wop = e(nc.sbuf_tensor("wop", [128, 16 * DM], BF16))
        wv = e(nc.sbuf_tensor("wv", [128, 8 * DM], BF16))
        gsc = e(nc.sbuf_tensor("gsc", [128, 8 * 512], BF16))
        ygc = e(nc.sbuf_tensor("ygc", [128, 2 * 16 * 512], BF16))
        xtc = e(nc.sbuf_tensor("xtc", [128, 4 * DM], F32))
        xfm = e(nc.sbuf_tensor("xfm", [128, 8 * 512], F32))
        x1c = e(nc.sbuf_tensor("x1c", [128, 8 * 512], F32))
        sq1 = e(nc.sbuf_tensor("sq1", [128, 8 * 512], F32))
        xn1c = e(nc.sbuf_tensor("xn1c", [128, 8 * 512], BF16))
        rowbuf = e(nc.sbuf_tensor("rowbuf", [1, 6 * 512], F32))
        vtm = e(nc.sbuf_tensor("vtm", [128, 4 * DM], BF16))
        ps = [e(nc.psum_tensor(f"ps{j}", [128, 512], F32)) for j in range(2)]
        pt = e(nc.psum_tensor("pt", [128, 128], F32))
        pr = e(nc.psum_tensor("pr", [1, 512], F32))
        pq = e(nc.psum_tensor("pq", [1, 512], F32))
        pbc = e(nc.psum_tensor("pbc", [128, 512], F32))
        pv = [e(nc.psum_tensor(f"pv{j}", [128, 512], F32)) for j in range(2)]

        w_wop = []
        for f in range(16):
            w_wop.append(p.dma("gpsimd", lambda h, f=f: h.dma_start(
                out=wop[:, f * DM:(f + 1) * DM], in_=wopT[f * 128:(f + 1) * 128, :])))
        w_wv = []
        for f in range(8):
            w_wv.append(p.dma("gpsimd", lambda h, f=f: h.dma_start(
                out=wv[:, f * DM:(f + 1) * DM], in_=wvT[f * 128:(f + 1) * 128, :])))
        g_r = g0.rearrange("(a p) t -> p a t", p=128)

        w_ygfree = [None, None]
        w_gc_free = None
        pev = [None, None]
        ptev = None
        prev_row = None
        pbcev = None
        pvev = [None, None]
        w_xtc_free = None
        w_xfm_free = None
        w_x1_free = None
        w_xn1_free = None
        w_xn1_free_v = None
        w_v_free = None
        for ci in range(8):
            c0 = ci * 512
            mb = ci % 2
            ygv = ygc[:, mb * 16 * 512:(mb + 1) * 16 * 512]
            w_yg = p.dma("sync", lambda h, c0=c0, ygv=ygv: h.dma_start(
                out=ygv.rearrange("p (a t) -> p a t", a=16),
                in_=yg_r[:, :, c0:c0 + 512]),
                waits=[w_ygfree[mb]] if w_ygfree[mb] else [])
            w_gc = p.dma("gpsimd", lambda h, c0=c0: h.dma_start(
                out=gsc[:, :].rearrange("p (a t) -> p a t", a=8),
                in_=g_r[:, :, c0:c0 + 512]),
                waits=[w_gc_free] if w_gc_free else [])
            w_xt = p.dma("sync", lambda h, ci=ci: h.dma_start(
                out=xtc[:, :].rearrange("p (a d) -> p a d", a=4),
                in_=x_tm.rearrange("(a p) d -> p a d", p=128)[:, 4 * ci:4 * ci + 4, :]),
                waits=[w_xtc_free] if w_xtc_free else [])
            w_tp = []
            for ts_ in range(4):
                for f in range(8):
                    tw = [w_xt] + ([ptev] if ptev else [])
                    w1 = p.op("tensor", lambda h, ts_=ts_, f=f: h.transpose(
                        pt[:, :], xtc[:, ts_ * DM + f * 128:ts_ * DM + (f + 1) * 128],
                        identf[:, :]), waits=tw)
                    w2 = p.op("vector", lambda h, ts_=ts_, f=f: h.tensor_copy(
                        xfm[:, f * 512 + ts_ * 128:f * 512 + (ts_ + 1) * 128],
                        pt[:, :]), waits=[w1] + ([w_xfm_free] if (ts_ == 0 and f == 0 and w_xfm_free) else []))
                    ptev = w2
                    w_tp.append(w2)
            w_s_all = []
            for fo in range(8):
                bank = fo % 2
                mm = None
                for fi in range(16):
                    tw = []
                    if fi == 0:
                        tw = [w_yg, w_wop[15]]
                        if pev[bank] is not None:
                            tw.append(pev[bank])
                    mm = p.op("tensor", lambda h, fi=fi, fo=fo, ygv=ygv, bank=bank:
                              h.matmul(ps[bank][:, :],
                                       wop[:, fi * DM + fo * 128:fi * DM + (fo + 1) * 128],
                                       ygv[:, fi * 512:(fi + 1) * 512],
                                       start=(fi == 0), stop=(fi == 15)), waits=tw)
                w_m = p.op("vector", lambda h, fo=fo, bank=bank: h.tensor_tensor(
                    x1c[:, fo * 512:(fo + 1) * 512], ps[bank][:, :],
                    gsc[:, fo * 512:(fo + 1) * 512], ALU.mult),
                    waits=[mm, w_gc, w_tp[-1]] + ([w_x1_free] if (fo == 0 and w_x1_free) else []))
                pev[bank] = w_m
                pev[bank] = w_m
                w_a = p.op("vector", lambda h, fo=fo: h.tensor_tensor(
                    x1c[:, fo * 512:(fo + 1) * 512], x1c[:, fo * 512:(fo + 1) * 512],
                    xfm[:, fo * 512:(fo + 1) * 512], ALU.add), waits=[w_m])
                w_s_all.append(w_a)
                w_x1dma = p.dma("gpsimd", lambda h, fo=fo, c0=c0: h.dma_start(
                    out=x1_o[fo * 128:(fo + 1) * 128, c0:c0 + 512],
                    in_=x1c[:, fo * 512:(fo + 1) * 512]), waits=[w_a])
            w_sq = p.op("scalar", lambda h: h.activation(
                sq1[:, :], x1c[:, :], AF.Square), waits=w_s_all)
            tw = [w_s_all[-1]] + ([prev_row] if prev_row else [])
            mmr = None
            for f in range(8):
                mmr = p.op("tensor", lambda h, f=f: h.matmul(
                    pr[:, :], onesc[:, :], x1c[:, f * 512:(f + 1) * 512],
                    start=(f == 0), stop=(f == 7)), waits=tw if f == 0 else [])
            mmq = None
            for f in range(8):
                mmq = p.op("tensor", lambda h, f=f: h.matmul(
                    pq[:, :], onesc[:, :], sq1[:, f * 512:(f + 1) * 512],
                    start=(f == 0), stop=(f == 7)),
                    waits=[w_sq] + (tw if f == 0 else []))
            rA = rowbuf[:, 0:512]; rB = rowbuf[:, 512:1024]
            rm = rowbuf[:, 1024:1536]; rv = rowbuf[:, 1536:2048]
            rsd = rowbuf[:, 2048:2560]
            w_rm = p.op("vector", lambda h: h.tensor_scalar(
                rm, pr[:, :], 1.0 / DM, None, ALU.mult), waits=[mmr])
            w_rv = p.op("vector", lambda h: h.tensor_scalar(
                rv, pq[:, :], 1.0 / DM, None, ALU.mult), waits=[mmq])
            w_m2 = p.op("vector", lambda h: h.scalar_tensor_tensor(
                rv, rm, 1.0, rm, ALU.bypass, ALU.mult) if False else
                h.scalar_tensor_tensor(rB, rm, 1.0, rm, ALU.mult, ALU.mult),
                waits=[w_rm])
            w_var = p.op("vector", lambda h: h.tensor_tensor(
                rv, rv, rB, ALU.subtract), waits=[w_rv, w_m2])
            w_sd = p.op("scalar", lambda h: h.activation(
                rsd, rv, AF.Sqrt, bias=C["epscol"][0:1, :]), waits=[w_var])
            w_rstd = p.op("vector", lambda h: h.reciprocal(rA, rsd), waits=[w_sd])
            w_rB = p.op("vector", lambda h: h.scalar_tensor_tensor(
                rB, rm, -1.0, rA, ALU.mult, ALU.mult), waits=[w_rstd])
            prev_row = w_rB
            tw = [w_rB] + ([pbcev] if pbcev else [])
            w_bA = p.op("tensor", lambda h: h.matmul(
                pbc[:, :], C["ones1"][:, :], rA, start=True, stop=True), waits=tw)
            w_cA = p.op("vector", lambda h: h.tensor_copy(sq1[:, 0:512], pbc[:, :]),
                        waits=[w_bA, w_sq])
            w_bB = p.op("tensor", lambda h: h.matmul(
                pbc[:, :], C["ones1"][:, :], rB, start=True, stop=True),
                waits=[w_cA])
            w_cB = p.op("vector", lambda h: h.tensor_copy(sq1[:, 512:1024], pbc[:, :]),
                        waits=[w_bB])
            pbcev = w_cB
            w_xn_all = []
            for fo in range(8):
                xw = [w_cA]
                if fo == 0 and w_xn1_free:
                    xw.append(w_xn1_free)
                if fo == 0 and w_xn1_free_v:
                    xw.append(w_xn1_free_v)
                w1 = p.op("vector", lambda h, fo=fo: h.scalar_tensor_tensor(
                    xn1c[:, fo * 512:(fo + 1) * 512], x1c[:, fo * 512:(fo + 1) * 512],
                    1.0, sq1[:, 0:512], ALU.mult, ALU.mult), waits=xw)
                w2 = p.op("vector", lambda h, fo=fo: h.tensor_tensor(
                    xn1c[:, fo * 512:(fo + 1) * 512], xn1c[:, fo * 512:(fo + 1) * 512],
                    sq1[:, 512:1024], ALU.add), waits=[w1, w_cB])
                w_xn_all.append(w2)
                w_xn1dma = p.dma("gpsimd", lambda h, fo=fo, c0=c0: h.dma_start(
                    out=xn1_o[fo * 128:(fo + 1) * 128, c0:c0 + 512],
                    in_=xn1c[:, fo * 512:(fo + 1) * 512]), waits=[w2])
            w_x1_free = w_x1dma
            w_xfm_free = w_s_all[-1]
            w_xtc_free = w_tp[-1]
            w_ygfree[mb] = w_s_all[-1]
            w_gc_free = w_s_all[-1]
            for ts_ in range(4):
                for vo in range(2):
                    bank = (ts_ * 2 + vo) % 2
                    mm = None
                    for fi in range(8):
                        tw = []
                        if fi == 0:
                            tw = [w_xn_all[-1], w_wv[7]]
                            if pvev[bank] is not None:
                                tw.append(pvev[bank])
                        mm = p.op("tensor", lambda h, fi=fi, ts_=ts_, vo=vo, bank=bank:
                                  h.matmul(pv[bank][:, :],
                                           xn1c[:, fi * 512 + ts_ * 128:
                                                fi * 512 + (ts_ + 1) * 128],
                                           wv[:, fi * DM + vo * 512:
                                              fi * DM + (vo + 1) * 512],
                                           start=(fi == 0), stop=(fi == 7)), waits=tw)
                    wv2 = p.op("vector", lambda h, ts_=ts_, vo=vo, bank=bank:
                               h.tensor_copy(vtm[:, ts_ * DM + vo * 512:
                                                 ts_ * DM + (vo + 1) * 512],
                                             pv[bank][:, :]),
                               waits=[mm] + ([w_v_free] if (ts_ == 0 and vo == 0 and w_v_free) else []))
                    pvev[bank] = wv2
                    w_vlast = wv2
            w_xn1_free_v = w_vlast
            w_xn1_free = w_xn1dma
            w_v_free = p.dma("gpsimd", lambda h, ci=ci: h.dma_start(
                out=v_o.rearrange("(c p) d -> p c d", p=128)[:, 4 * ci:4 * ci + 4, :],
                in_=vtm[:, :].rearrange("p (c d) -> p c d", c=4)), waits=[pvev[0], pvev[1]])
        p.end_stage("L4")
    return nc


def l4_in_maps(ins, l1r, l2r, l3r):
    wopT = np.ascontiguousarray(ins["L0_attn_out_proj_w"].T)
    wvT = np.ascontiguousarray(ins["L1_attn_W_v"].T)
    x = ins["L0_attn_hidden_states"]
    maps = []
    for c in range(8):
        b, hh = c // 2, c % 2
        t0 = hh * TH
        ygf = np.concatenate([
            l3r[2 * b]["yg_o"][:, t0:t0 + TH],
            l3r[2 * b + 1]["yg_o"][:, t0:t0 + TH]], axis=0)
        maps.append({
            "ygf": np.ascontiguousarray(ygf),
            "wopT": wopT,
            "x_tm": np.ascontiguousarray(x[b, t0:t0 + TH]),
            "g0": l2r[c]["g_o"],
            "wvT": wvT,
        })
    return maps


def build_L5():
    L = 8192
    nc = bass.Bass("TRN2", target_bir_lowering=False)
    xn1f = nc.dram_tensor("xn1f", [DM, L], BF16, kind="ExternalInput").ap()
    vf = nc.dram_tensor("vf", [L, DM], BF16, kind="ExternalInput").ap()
    myxn = nc.dram_tensor("myxn", [DM, TH], BF16, kind="ExternalInput").ap()
    wqk = nc.dram_tensor("wqk", [DM, DM], F32, kind="ExternalInput").ap()
    slopec = nc.dram_tensor("slopec", [128, 1], F32, kind="ExternalInput").ap()
    tqb = nc.dram_tensor("tqb", [128, 1], F32, kind="ExternalInput").ap()
    s1_o = nc.dram_tensor("s1_o", [DM, TH], F32, kind="ExternalOutput").ap()

    xn_r = xn1f.rearrange("(a p) t -> p a t", p=128)
    my_r = myxn.rearrange("(a p) t -> p a t", p=128)
    vf_r = vf.rearrange("(c p) d -> p c d", p=128)

    with ExitStack() as octx:
        oe = octx.enter_context
        p = Prog(nc, octx)
        C = emit_common_consts(p, oe, nc)
        Q = oe(nc.sbuf_tensor("Q", [128, 8 * TH], BF16))
        slc = oe(nc.sbuf_tensor("slc", [128, 1], F32))
        tqc = oe(nc.sbuf_tensor("tqc", [128, 1], F32))
        pf256 = oe(nc.sbuf_tensor("pf256", [128, 256], F32))
        slopef = oe(nc.sbuf_tensor("slopef", [128, 256], F32))
        pcol = oe(nc.sbuf_tensor("pcol", [128, 1], F32))
        ones128b = oe(nc.sbuf_tensor("ones128b", [128, 1], BF16))

        w_sl = p.dma("sync", lambda h: h.dma_start(out=slc[:, :], in_=slopec[:, :]))
        w_tq = p.dma("sync", lambda h: h.dma_start(out=tqc[:, :], in_=tqb[:, :]))
        w_i1 = p.op("gpsimd", lambda h: h.iota(pf256[:, :], [[-1, 256]], base=0,
                                               channel_multiplier=1,
                                               allow_small_or_imprecise_dtypes=True))
        w_i2 = p.op("gpsimd", lambda h: h.iota(slopef[:, :], [[1, 256]], base=0,
                                               channel_multiplier=0,
                                               allow_small_or_imprecise_dtypes=True))
        w_i3 = p.op("gpsimd", lambda h: h.iota(pcol[:, :], [[1, 1]], base=0,
                                               channel_multiplier=1,
                                               allow_small_or_imprecise_dtypes=True))
        w_sf = p.op("vector", lambda h: h.tensor_scalar(
            slopef[:, :], slopef[:, :], slc[:, :], -1.0, ALU.mult, ALU.mult),
            waits=[w_i2, w_sl])
        p.op("vector", lambda h: h.memset(ones128b[:, :], 1.0))

        with ExitStack() as ctx:
            e = ctx.enter_context
            wqs = e(nc.sbuf_tensor("wqs", [128, 8 * DM], BF16))
            mys = e(nc.sbuf_tensor("mys", [128, 8 * TH], BF16))
            pq_ = [e(nc.psum_tensor(f"pq_{j}", [128, 512], F32)) for j in range(2)]
            w_wq = []
            for f in range(8):
                w_wq.append(p.dma("gpsimd", lambda h, f=f: h.dma_start(
                    out=wqs[:, f * DM:(f + 1) * DM], in_=wqk[f * 128:(f + 1) * 128, :])))
            w_my = p.dma("sync", lambda h: h.dma_start(
                out=mys[:, :].rearrange("p (a t) -> p a t", a=8), in_=my_r[:, :, :]))
            pev = [None, None]
            for ci in range(8):
                for fo in range(8):
                    bank = (ci * 8 + fo) % 2
                    mm = None
                    for fi in range(8):
                        tw = []
                        if fi == 0:
                            tw = [w_my, w_wq[7]]
                            if pev[bank] is not None:
                                tw.append(pev[bank])
                        mm = p.op("tensor", lambda h, fi=fi, fo=fo, ci=ci, bank=bank:
                                  h.matmul(pq_[bank][:, :],
                                           wqs[:, fi * DM + fo * 128:fi * DM + (fo + 1) * 128],
                                           mys[:, fi * TH + ci * 512:fi * TH + (ci + 1) * 512],
                                           start=(fi == 0), stop=(fi == 7)), waits=tw)
                    wc = p.op("vector", lambda h, fo=fo, ci=ci, bank=bank: h.tensor_copy(
                        Q[:, fo * TH + ci * 512:fo * TH + (ci + 1) * 512],
                        pq_[bank][:, :]), waits=[mm])
                    pev[bank] = wc
            p.end_stage("L5Q")

        with ExitStack() as ctx:
            e = ctx.enter_context
            kst = e(nc.sbuf_tensor("kst", [128, 2 * 8 * 1024], BF16))
            vst = e(nc.sbuf_tensor("vst", [128, 2 * 8 * 1024], BF16))
            sef = e(nc.sbuf_tensor("sef", [128, 256], F32))
            dmask = e(nc.sbuf_tensor("dmask", [128, 256], F32))
            ccol = e(nc.sbuf_tensor("ccol", [128, 1], F32))
            ac1 = e(nc.sbuf_tensor("ac1", [128, 1], F32))
            acol = e(nc.sbuf_tensor("acol", [128, 1], F32))
            pt_ = e(nc.sbuf_tensor("pt_", [128, 256], BF16))
            dnr = e(nc.sbuf_tensor("dnr", [1, 256], F32))
            rcr = e(nc.sbuf_tensor("rcr", [1, 256], F32))
            rbc = e(nc.sbuf_tensor("rbc", [128, 256], F32))
            s1c = e(nc.sbuf_tensor("s1c", [128, 256], F32))
            pvp = [e(nc.psum_tensor(f"pvp{j}", [128, 512], F32)) for j in range(4)]
            stp = e(nc.psum_tensor("stp", [128, 256], F32))
            dnp = e(nc.psum_tensor("dnp", [1, 256], F32))
            auxp = e(nc.psum_tensor("auxp", [128, 256], F32))

            w_kfree = [None, None]
            w_vfree = [None, None]
            w_stfree = None
            w_ptfree = None
            w_dnfree = None
            w_pvfree = None
            w_auxfree = None
            w_seffree = None
            for qc in range(16):
                q0 = qc * 256
                last_pv = None
                last_dn = None
                for tkg in range(8):
                    gb = tkg % 2
                    kv = kst[:, gb * 8 * 1024:(gb + 1) * 8 * 1024]
                    vv = vst[:, gb * 8 * 1024:(gb + 1) * 8 * 1024]
                    klw = [w_kfree[gb]] if w_kfree[gb] else []
                    w_k = p.dma("sync", lambda h, tkg=tkg, kv=kv: h.dma_start(
                        out=kv.rearrange("p (a t) -> p a t", a=8),
                        in_=xn_r[:, :, tkg * 1024:(tkg + 1) * 1024]), waits=klw)
                    vlw = [w_vfree[gb]] if w_vfree[gb] else []
                    w_v = p.dma("scalar", lambda h, tkg=tkg, vv=vv: h.dma_start(
                        out=vv.rearrange("p (c d) -> p c d", c=8),
                        in_=vf_r[:, 8 * tkg:8 * tkg + 8, :]), waits=vlw)
                    for tb in range(8):
                        tkb = tkg * 8 + tb
                        tk0 = tkb * 128
                        mm = None
                        for fi in range(8):
                            tw = []
                            if fi == 0:
                                tw = [w_k]
                                if w_stfree is not None:
                                    tw.append(w_stfree)
                            mm = p.op("tensor", lambda h, fi=fi, tb=tb, kv=kv, q0=q0:
                                      h.matmul(stp[:, :],
                                               kv[:, fi * 1024 + tb * 128:
                                                  fi * 1024 + (tb + 1) * 128],
                                               Q[:, fi * TH + q0:fi * TH + q0 + 256],
                                               start=(fi == 0), stop=(fi == 7)), waits=tw)
                        w_se = p.op("vector", lambda h: h.scalar_tensor_tensor(
                            sef[:, :], stp[:, :], 0.03125, slopef[:, :],
                            ALU.mult, ALU.add),
                            waits=[mm, w_sf] + ([w_seffree] if w_seffree else []))
                        w_stfree = w_se
                        w_cc = p.op("vector", lambda h, tk0=tk0, q0=q0: h.tensor_scalar(
                            ccol[:, :], tqc[:, :], -1.0, float(tk0 - q0),
                            ALU.mult, ALU.add), waits=[w_tq, w_i1])
                        w_dm = p.op("vector", lambda h: h.tensor_scalar(
                            dmask[:, :], pf256[:, :], ccol[:, :], None, ALU.add),
                            waits=[w_cc])
                        w_dm2 = p.op("vector", lambda h: h.tensor_scalar(
                            dmask[:, :], dmask[:, :], 1.0, 0.0, ALU.min, ALU.max),
                            waits=[w_dm])
                        w_se2 = p.op("vector", lambda h: h.scalar_tensor_tensor(
                            sef[:, :], dmask[:, :], -1e30, sef[:, :],
                            ALU.mult, ALU.add), waits=[w_dm2, w_se])
                        w_a1 = p.op("vector", lambda h, tk0=tk0, q0=q0: h.tensor_scalar(
                            ac1[:, :], pcol[:, :], 1.0, float(tk0 - q0),
                            ALU.mult, ALU.add), waits=[w_i3])
                        w_a2 = p.op("vector", lambda h: h.tensor_tensor(
                            ac1[:, :], ac1[:, :], tqc[:, :], ALU.subtract), waits=[w_a1])
                        w_a3 = p.op("vector", lambda h: h.tensor_tensor(
                            acol[:, :], ac1[:, :], slc[:, :], ALU.mult), waits=[w_a2])
                        w_p = p.op("scalar", lambda h: h.activation(
                            pt_[:, :], sef[:, :], AF.Exp, bias=acol[:, :]),
                            waits=[w_se2, w_a3] + ([w_ptfree] if w_ptfree else []))
                        w_seffree = w_p
                        lastm = None
                        for ds in range(8):
                            tw = [w_p, w_v]
                            if tkb == 0 and w_pvfree is not None and ds == 0:
                                tw.append(w_pvfree)
                            lastm = p.op("tensor", lambda h, ds=ds, tb=tb, vv=vv, tkb=tkb:
                                         h.matmul(pvp[ds // 2][:, (ds % 2) * 256:
                                                               (ds % 2) * 256 + 256],
                                                  vv[:, tb * 1024 + ds * 128:
                                                     tb * 1024 + (ds + 1) * 128],
                                                  pt_[:, :],
                                                  start=(tkb == 0 and ds % 2 == 0),
                                                  stop=(tkb == 63)),
                                         waits=tw)
                        w_dn = p.op("tensor", lambda h, tkb=tkb: h.matmul(
                            dnp[:, :], ones128b[:, :], pt_[:, :],
                            start=(tkb == 0), stop=(tkb == 63)),
                            waits=[w_p] + ([w_dnfree] if (tkb == 0 and w_dnfree) else []))
                        w_ptfree = w_dn
                        last_pv = lastm
                        last_dn = w_dn
                    w_kfree[gb] = last_pv
                    w_vfree[gb] = last_pv
                w_d1 = p.op("vector", lambda h: h.tensor_copy(dnr[:, :], dnp[:, :]),
                            waits=[last_dn])
                w_dnfree = w_d1
                w_d2 = p.op("vector", lambda h: h.reciprocal(rcr[:, :], dnr[:, :]),
                            waits=[w_d1])
                tw = [w_d2] + ([w_auxfree] if w_auxfree else [])
                w_b = p.op("tensor", lambda h: h.matmul(
                    auxp[:, :], C["ones1"][:, :], rcr[:, :], start=True, stop=True),
                    waits=tw)
                w_rb = p.op("vector", lambda h: h.tensor_copy(rbc[:, :], auxp[:, :]),
                            waits=[w_b])
                w_auxfree = w_rb
                w_pvlast = None
                for ds in range(8):
                    w_s1 = p.op("vector", lambda h, ds=ds: h.tensor_tensor(
                        s1c[:, :], pvp[ds // 2][:, (ds % 2) * 256:(ds % 2) * 256 + 256],
                        rbc[:, :], ALU.mult), waits=[w_rb, last_pv] +
                        ([w_pvlast] if w_pvlast else []))
                    w_pvlast = p.dma("gpsimd", lambda h, ds=ds, q0=q0: h.dma_start(
                        out=s1_o[ds * 128:(ds + 1) * 128, q0:q0 + 256],
                        in_=s1c[:, :]), waits=[w_s1])
                w_pvfree = w_pvlast
            p.end_stage("L5A")
    return nc


def l5_in_maps(ins, l4r):
    slope = np.float32(ins["L1_attn_alibi_slope"][0])
    maps = []
    for c in range(8):
        b, hh = c // 2, c % 2
        t0 = hh * TH
        xn1f = np.concatenate([l4r[2 * b]["xn1_o"], l4r[2 * b + 1]["xn1_o"]], axis=1)
        vfull = np.concatenate([l4r[2 * b]["v_o"], l4r[2 * b + 1]["v_o"]], axis=0)
        maps.append({
            "xn1f": np.ascontiguousarray(xn1f),
            "vf": np.ascontiguousarray(vfull),
            "myxn": l4r[c]["xn1_o"],
            "wqk": ins["L1_attn_W_qk"],
            "slopec": np.full((128, 1), slope, np.float32),
            "tqb": np.full((128, 1), np.float32(t0), np.float32),
        })
    return maps


def build_L6():
    E = 8
    nc = bass.Bass("TRN2", target_bir_lowering=False)
    myxn = nc.dram_tensor("myxn", [DM, TH], BF16, kind="ExternalInput").ap()
    x1 = nc.dram_tensor("x1", [DM, TH], F32, kind="ExternalInput").ap()
    s1 = nc.dram_tensor("s1", [DM, TH], F32, kind="ExternalInput").ap()
    wp = nc.dram_tensor("wp", [DM, DM], F32, kind="ExternalInput").ap()
    bpc = nc.dram_tensor("bpc", [128, 8], F32, kind="ExternalInput").ap()
    cent = nc.dram_tensor("cent", [E, DM], F32, kind="ExternalInput").ap()
    w1 = nc.dram_tensor("w1", [E, DM, 4096], F32, kind="ExternalInput").ap()
    b1c = nc.dram_tensor("b1c", [128, 256], F32, kind="ExternalInput").ap()
    w2 = nc.dram_tensor("w2", [E, 4096, DM], F32, kind="ExternalInput").ap()
    b2m = nc.dram_tensor("b2m", [E, DM], F32, kind="ExternalInput").ap()
    y_o = nc.dram_tensor("y_o", [TH, DM], F32, kind="ExternalOutput").ap()
    moe_d = nc.dram_tensor("moe_d", [DM, TH], F32)

    my_r = myxn.rearrange("(a p) t -> p a t", p=128)
    x1_r = x1.rearrange("(a p) t -> p a t", p=128)
    s1_r = s1.rearrange("(a p) t -> p a t", p=128)
    moe_r = moe_d.rearrange("(a p) t -> p a t", p=128)

    EPS = 1e-07
    with ExitStack() as octx:
        oe = octx.enter_context
        p = Prog(nc, octx)
        C = emit_common_consts(p, oe, nc)
        combfm = oe(nc.sbuf_tensor("combfm", [E, TH], BF16))
        SEL = oe(nc.sbuf_tensor("SEL", [E, E * 128], BF16))
        selpf = oe(nc.sbuf_tensor("selpf", [E, E * 128], F32))
        onescol = oe(nc.sbuf_tensor("onescol", [128, 1], F32))
        w_selpf = p.op("gpsimd", lambda h: h.iota(
            selpf[:, :].rearrange("p (e m) -> p e m", e=E), [[-1, E], [0, 128]],
            base=0, channel_multiplier=1, allow_small_or_imprecise_dtypes=True))
        w_sel = p.op("vector", lambda h: h.tensor_scalar(
            SEL[:, :], selpf[:, :], 0.0, None, ALU.is_equal), waits=[w_selpf])
        p.op("vector", lambda h: h.memset(onescol[:, :], 1.0))

        with ExitStack() as ctx:
            e = ctx.enter_context
            wps = e(nc.sbuf_tensor("wps", [128, 8 * DM], BF16))
            bps = e(nc.sbuf_tensor("bps", [128, 8], F32))
            cen = e(nc.sbuf_tensor("cen", [E, DM], F32))
            ccol8 = e(nc.sbuf_tensor("ccol8", [E, 4], F32))
            cpTb = e(nc.sbuf_tensor("cpTb", [128, 8 * E], BF16))
            cnsrow = e(nc.sbuf_tensor("cnsrow", [1, E], F32))
            c1mrow = e(nc.sbuf_tensor("c1mrow", [1, E], F32))
            CNSB = e(nc.sbuf_tensor("CNSB", [128, E], F32))
            C1MB = e(nc.sbuf_tensor("C1MB", [128, E], F32))
            myc = e(nc.sbuf_tensor("myc", [128, 8 * 512], BF16))
            zh = e(nc.sbuf_tensor("zh", [128, 8 * 512], F32))
            sqz = e(nc.sbuf_tensor("sqz", [128, 8 * 512], F32))
            xhc = e(nc.sbuf_tensor("xhc", [128, 8 * 512], BF16))
            rows = e(nc.sbuf_tensor("rows", [1, 8 * 512], F32))
            scb = e(nc.sbuf_tensor("scb", [128, 512], F32))
            xnscols = e(nc.sbuf_tensor("xnscols", [128, 4], F32))
            gt = e(nc.sbuf_tensor("gt", [128, 16 * E], F32))
            gcol = e(nc.sbuf_tensor("gcol", [128, 8], F32))
            pz = [e(nc.psum_tensor(f"pz{j}", [128, 512], F32)) for j in range(2)]
            pro = e(nc.psum_tensor("pro", [1, 512], F32))
            pbb = e(nc.psum_tensor("pbb", [128, 512], F32))
            pd2 = e(nc.psum_tensor("pd2", [128, E], F32))
            ptc = e(nc.psum_tensor("ptc", [128, 128], F32))

            w_wp = []
            for f in range(8):
                w_wp.append(p.dma("gpsimd", lambda h, f=f: h.dma_start(
                    out=wps[:, f * DM:(f + 1) * DM], in_=wp[f * 128:(f + 1) * 128, :])))
            w_bp = p.dma("sync", lambda h: h.dma_start(out=bps[:, :], in_=bpc[:, :]))
            w_ce = p.dma("sync", lambda h: h.dma_start(out=cen[:, :], in_=cent[:, :]))

            w_csq = p.op("scalar", lambda h: h.activation(
                sqz[0:E, 0:DM], cen[:, :], AF.Square), waits=[w_ce])
            w_cn2 = p.op("vector", lambda h: h.tensor_reduce(
                ccol8[:, 0:1], sqz[0:E, 0:DM], AX.X, ALU.add), waits=[w_csq])
            w_cnm = p.op("scalar", lambda h: h.activation(
                ccol8[:, 1:2], ccol8[:, 0:1], AF.Sqrt), waits=[w_cn2])
            w_crc = p.op("vector", lambda h: h.reciprocal(ccol8[:, 2:3], ccol8[:, 1:2]),
                         waits=[w_cnm])
            w_csc = p.op("vector", lambda h: h.tensor_scalar(
                ccol8[:, 3:4], ccol8[:, 2:3], 0.95, 1.0, ALU.mult, ALU.min),
                waits=[w_crc])
            w_cp = p.op("vector", lambda h: h.tensor_scalar(
                cen[:, :], cen[:, :], ccol8[:, 3:4], None, ALU.mult), waits=[w_csc])
            w_cq2 = p.op("scalar", lambda h: h.activation(
                sqz[0:E, 0:DM], cen[:, :], AF.Square), waits=[w_cp, w_cn2])
            w_cn3 = p.op("vector", lambda h: h.tensor_reduce(
                ccol8[:, 0:1], sqz[0:E, 0:DM], AX.X, ALU.add), waits=[w_cq2])
            w_cns = p.op("vector", lambda h: h.tensor_scalar(
                ccol8[:, 0:1], ccol8[:, 0:1], 0.99, None, ALU.min), waits=[w_cn3])
            w_cpb = p.op("vector", lambda h: h.tensor_copy(
                sqz[0:E, 0:DM], cen[:, :]), waits=[w_cp, w_cn3])
            w_tpc = []
            for f in range(8):
                tw = [w_cpb] + ([w_tpc[-1]] if w_tpc else [])
                w1_ = p.op("tensor", lambda h, f=f: h.transpose(
                    ptc[:, 0:E], sqz[0:E, f * 128:(f + 1) * 128],
                    C["identf"][0:E, 0:E]), waits=tw)
                w2_ = p.op("vector", lambda h, f=f: h.tensor_copy(
                    cpTb[:, f * E:(f + 1) * E], ptc[:, 0:E]), waits=[w1_])
                w_tpc.append(w2_)
            w_t1 = p.op("tensor", lambda h: h.transpose(
                ptc[0:1, 0:E], ccol8[0:E, 0:1], C["identf"][0:E, 0:E]),
                waits=[w_cns, w_tpc[-1]])
            w_t2 = p.op("vector", lambda h: h.tensor_copy(cnsrow[:, :], ptc[0:1, 0:E]),
                        waits=[w_t1])
            w_t3 = p.op("vector", lambda h: h.tensor_scalar(
                c1mrow[:, :], cnsrow[:, :], -1.0, 1.0, ALU.mult, ALU.add), waits=[w_t2])
            w_b1 = p.op("tensor", lambda h: h.matmul(
                pd2[:, :], C["ones1"][:, :], cnsrow[:, :], start=True, stop=True),
                waits=[w_t3])
            w_b2 = p.op("vector", lambda h: h.tensor_copy(CNSB[:, :], pd2[:, :]),
                        waits=[w_b1])
            w_b3 = p.op("tensor", lambda h: h.matmul(
                pd2[:, :], C["ones1"][:, :], c1mrow[:, :], start=True, stop=True),
                waits=[w_b2])
            w_b4 = p.op("vector", lambda h: h.tensor_copy(C1MB[:, :], pd2[:, :]),
                        waits=[w_b3])

            pev = [None, None]
            w_pro_free = None
            w_pbb_free = None
            w_pd2_free = w_b4
            w_myc_free = None
            w_zh_free = None
            w_xh_free = None
            w_comb_free = None
            for ci in range(8):
                c0 = ci * 512
                w_my = p.dma("sync", lambda h, ci=ci: h.dma_start(
                    out=myc[:, :].rearrange("p (a t) -> p a t", a=8),
                    in_=my_r[:, :, ci * 512:(ci + 1) * 512]),
                    waits=[w_myc_free] if w_myc_free else [])
                w_zf = []
                for fo in range(8):
                    bank = fo % 2
                    mm = None
                    for fi in range(8):
                        tw = []
                        if fi == 0:
                            tw = [w_my, w_wp[7]]
                            if pev[bank] is not None:
                                tw.append(pev[bank])
                        mm = p.op("tensor", lambda h, fi=fi, fo=fo, bank=bank: h.matmul(
                            pz[bank][:, :],
                            wps[:, fi * DM + fo * 128:fi * DM + (fo + 1) * 128],
                            myc[:, fi * 512:(fi + 1) * 512],
                            start=(fi == 0), stop=(fi == 7)), waits=tw)
                    wz = p.op("vector", lambda h, fo=fo, bank=bank: h.tensor_scalar(
                        zh[:, fo * 512:(fo + 1) * 512], pz[bank][:, :],
                        bps[:, fo:fo + 1], None, ALU.add),
                        waits=[mm, w_bp] + ([w_zh_free] if (fo == 0 and w_zh_free) else []))
                    pev[bank] = wz
                    w_zf.append(wz)
                w_sq = p.op("scalar", lambda h: h.activation(
                    sqz[:, :], zh[:, :], AF.Square),
                    waits=w_zf + ([w_tpc[-1], w_cn3] if ci == 0 else []))
                mmr = None
                for f in range(8):
                    tw = [w_sq] if f > 0 else ([w_sq] + ([w_pro_free] if w_pro_free else []))
                    mmr = p.op("tensor", lambda h, f=f: h.matmul(
                        pro[:, :], onescol[:, :], sqz[:, f * 512:(f + 1) * 512],
                        start=(f == 0), stop=(f == 7)), waits=tw)
                rN = rows[:, 0:512]; rSD = rows[:, 512:1024]
                rRC = rows[:, 1024:1536]; rSC = rows[:, 1536:2048]
                rXN = rows[:, 2048:2560]
                w_rn = p.op("vector", lambda h: h.tensor_copy(rN, pro[:, :]), waits=[mmr])
                w_pro_free = w_rn
                w_rsd = p.op("scalar", lambda h: h.activation(rSD, rN, AF.Sqrt),
                             waits=[w_rn])
                w_rrc = p.op("vector", lambda h: h.reciprocal(rRC, rSD), waits=[w_rsd])
                w_rsc = p.op("vector", lambda h: h.tensor_scalar(
                    rSC, rRC, 0.95, 1.0, ALU.mult, ALU.min), waits=[w_rrc])
                w_x1_ = p.op("vector", lambda h: h.tensor_tensor(
                    rXN, rSC, rSC, ALU.mult), waits=[w_rsc])
                w_x2_ = p.op("vector", lambda h: h.tensor_tensor(
                    rXN, rXN, rN, ALU.mult), waits=[w_x1_])
                w_x3_ = p.op("vector", lambda h: h.tensor_scalar(
                    rXN, rXN, 0.99, None, ALU.min), waits=[w_x2_])
                tw = [w_rsc] + ([w_pbb_free] if w_pbb_free else [])
                w_sb = p.op("tensor", lambda h: h.matmul(
                    pbb[:, :], C["ones1"][:, :], rSC, start=True, stop=True), waits=tw)
                w_scb = p.op("vector", lambda h: h.tensor_copy(scb[:, :], pbb[:, :]),
                             waits=[w_sb])
                w_pbb_free = w_scb
                w_xh = []
                for fo in range(8):
                    w_ = p.op("vector", lambda h, fo=fo: h.tensor_tensor(
                        xhc[:, fo * 512:(fo + 1) * 512], zh[:, fo * 512:(fo + 1) * 512],
                        scb[:, :], ALU.mult),
                        waits=[w_scb] + ([w_xh_free] if (fo == 0 and w_xh_free) else []))
                    w_xh.append(w_)
                w_zh_free = w_xh[-1]
                w_xc = []
                for ts4 in range(4):
                    tw = [w_x3_, w_xh[-1]] + ([w_pd2_free] if w_pd2_free else [])
                    w1_ = p.op("tensor", lambda h, ts4=ts4: h.transpose(
                        ptc[:, 0:1],
                        rows[0:1, 2048 + ts4 * 128:2048 + (ts4 + 1) * 128],
                        C["identf"][0:1, 0:1]),
                        waits=[w_x3_] + ([w_xc[-1]] if w_xc else
                              ([w_t2] if ci == 0 else [w_comb_free])))
                    w2_ = p.op("vector", lambda h, ts4=ts4: h.tensor_copy(
                        xnscols[:, ts4:ts4 + 1], ptc[:, 0:1]), waits=[w1_])
                    w_xc.append(w2_)
                for ts4 in range(4):
                    mm = None
                    for fi in range(8):
                        tw = []
                        if fi == 0:
                            tw = [w_xh[-1], w_tpc[-1]]
                            if w_pd2_free is not None:
                                tw.append(w_pd2_free)
                        mm = p.op("tensor", lambda h, fi=fi, ts4=ts4: h.matmul(
                            pd2[:, :],
                            xhc[:, fi * 512 + ts4 * 128:fi * 512 + (ts4 + 1) * 128],
                            cpTb[:, fi * E:(fi + 1) * E],
                            start=(fi == 0), stop=(fi == 7)), waits=tw)
                    g0_ = gt[:, 0:E]; g1_ = gt[:, E:2 * E]; g2_ = gt[:, 2 * E:3 * E]
                    w_d1 = p.op("vector", lambda h: h.scalar_tensor_tensor(
                        g0_, pd2[:, :], -2.0, CNSB[:, :], ALU.mult, ALU.add),
                        waits=[mm, w_b2, w_comb_free or (p.sems["vector"], 0)])
                    w_pd2_free = w_d1
                    w_d2 = p.op("vector", lambda h, ts4=ts4: h.tensor_scalar(
                        g0_, g0_, xnscols[:, ts4:ts4 + 1], 0.0, ALU.add, ALU.max),
                        waits=[w_d1, w_xc[ts4]])
                    w_d3 = p.op("vector", lambda h, ts4=ts4: h.tensor_scalar(
                        gcol[:, 0:1], xnscols[:, ts4:ts4 + 1], -1.0, 1.0,
                        ALU.mult, ALU.add), waits=[w_xc[ts4], w_d1])
                    w_d4 = p.op("vector", lambda h: h.tensor_scalar(
                        g1_, C1MB[:, :], gcol[:, 0:1], EPS, ALU.mult, ALU.add),
                        waits=[w_d3, w_b4])
                    w_d5 = p.op("vector", lambda h: h.reciprocal(g1_, g1_), waits=[w_d4])
                    w_d6 = p.op("vector", lambda h: h.scalar_tensor_tensor(
                        g0_, g0_, 2.0, g1_, ALU.mult, ALU.mult), waits=[w_d5, w_d2])
                    w_d7 = p.op("vector", lambda h: h.tensor_scalar(
                        g0_, g0_, EPS, None, ALU.max), waits=[w_d6])
                    w_d8 = p.op("vector", lambda h: h.tensor_scalar(
                        g1_, g0_, 2.0, None, ALU.add), waits=[w_d7])
                    w_d9 = p.op("vector", lambda h: h.tensor_tensor(
                        g1_, g1_, g0_, ALU.mult), waits=[w_d8])
                    w_da = p.op("scalar", lambda h: h.activation(
                        g1_, g1_, AF.Sqrt), waits=[w_d9])
                    w_db = p.op("vector", lambda h: h.tensor_tensor(
                        g1_, g1_, g0_, ALU.add), waits=[w_da])
                    w_dc = p.op("scalar", lambda h: h.activation(
                        g1_, g1_, AF.Ln, bias=onescol[:, :]), waits=[w_db])
                    w_dd = p.op("vector", lambda h: h.tensor_scalar(
                        g1_, g1_, -1.0, None, ALU.mult), waits=[w_dc])
                    w_m8 = p.op("vector", lambda h: h.max(g2_, g1_), waits=[w_dd])
                    w_wd = p.op("vector", lambda h: h.tensor_tensor(
                        gcol[:, 1:2], g2_[:, 0:1], g2_[:, 1:2], ALU.subtract),
                        waits=[w_m8])
                    w_w1 = p.op("scalar", lambda h: h.activation(
                        gcol[:, 2:3], gcol[:, 1:2], AF.Sigmoid), waits=[w_wd])
                    w_w2 = p.op("vector", lambda h: h.tensor_scalar(
                        gcol[:, 3:4], gcol[:, 2:3], -1.0, 1.0, ALU.mult, ALU.add),
                        waits=[w_w1])
                    w_e1 = p.op("vector", lambda h: h.tensor_scalar(
                        gt[:, 3 * E:4 * E], g1_, g2_[:, 0:1], None, ALU.is_equal),
                        waits=[w_m8, w_w1])
                    w_e2 = p.op("vector", lambda h: h.tensor_scalar(
                        gt[:, 4 * E:5 * E], g1_, g2_[:, 1:2], None, ALU.is_equal),
                        waits=[w_e1])
                    w_e3 = p.op("vector", lambda h: h.tensor_scalar(
                        gt[:, 4 * E:5 * E], gt[:, 4 * E:5 * E], gcol[:, 3:4], None,
                        ALU.mult), waits=[w_e2, w_w2])
                    w_cmb = p.op("vector", lambda h: h.scalar_tensor_tensor(
                        gt[:, 5 * E:6 * E], gt[:, 3 * E:4 * E], gcol[:, 2:3],
                        gt[:, 4 * E:5 * E], ALU.mult, ALU.add), waits=[w_e3])
                    w_tc = p.op("tensor", lambda h: h.transpose(
                        ptc[0:E, 0:128], gt[:, 5 * E:6 * E], C["identf"][:, :]),
                        waits=[w_cmb, w_xc[3]] + ([w_comb_free] if w_comb_free else []))
                    w_cc2 = p.op("vector", lambda h, ci=ci, ts4=ts4: h.tensor_copy(
                        combfm[0:E, ci * 512 + ts4 * 128:ci * 512 + (ts4 + 1) * 128],
                        ptc[0:E, 0:128]), waits=[w_tc])
                    w_comb_free = w_cc2
                w_myc_free = w_zf[-1]
                w_xh_free = w_comb_free
            p.end_stage("L6G")

        with ExitStack() as ctx:
            e = ctx.enter_context
            w1s = e(nc.sbuf_tensor("w1s", [128, 8 * 4096], BF16))
            w2s = e(nc.sbuf_tensor("w2s", [128, 32 * DM], BF16))
            b1s = e(nc.sbuf_tensor("b1s", [128, 256], F32))
            b2s = e(nc.sbuf_tensor("b2s", [E, DM], BF16))
            myc2 = e(nc.sbuf_tensor("myc2", [128, 8 * 512], BF16))
            hbuf = e(nc.sbuf_tensor("hbuf", [128, 32 * 512], BF16))
            cbb = e(nc.sbuf_tensor("cbb", [128, 512], BF16))
            ysb = e(nc.sbuf_tensor("ysb", [128, 2 * 512], F32))
            ph = [e(nc.psum_tensor(f"ph{j}", [128, 512], F32)) for j in range(2)]
            py = [e(nc.psum_tensor(f"py{j}", [128, 512], F32)) for j in range(2)]
            pxb = e(nc.psum_tensor("pxb", [128, 512], F32))

            w_b1 = p.dma("sync", lambda h: h.dma_start(out=b1s[:, :], in_=b1c[:, :]))
            w_b2 = p.dma("gpsimd", lambda h: h.dma_start(out=b2s[:, :], in_=b2m[:, :]))
            w_w1free = None
            w_w2free = None
            w_myfree = None
            w_cbbfree = None
            phev = [None, None]
            pyev = [None, None]
            w_ysbfree = [None, None]
            w_hfree = None
            for ei in range(E):
                w_w1 = []
                for f in range(8):
                    w_w1.append(p.dma("gpsimd", lambda h, ei=ei, f=f: h.dma_start(
                        out=w1s[:, f * 4096:(f + 1) * 4096],
                        in_=w1[ei, f * 128:(f + 1) * 128, :]),
                        waits=[w_w1free] if (f == 0 and w_w1free) else []))
                w_w2 = []
                for f in range(32):
                    w_w2.append(p.dma("gpsimd", lambda h, ei=ei, f=f: h.dma_start(
                        out=w2s[:, f * DM:(f + 1) * DM],
                        in_=w2[ei, f * 128:(f + 1) * 128, :]),
                        waits=[w_w2free] if (f == 0 and w_w2free) else []))
                for ci in range(8):
                    c0 = ci * 512
                    tw = [w_comb_free, w_sel]
                    if w_cbbfree is not None:
                        tw.append(w_cbbfree)
                    w_cb1 = p.op("tensor", lambda h, ei=ei, ci=ci: h.matmul(
                        pxb[:, :], SEL[:, ei * 128:(ei + 1) * 128],
                        combfm[:, ci * 512:(ci + 1) * 512],
                        start=True, stop=True), waits=tw)
                    w_cb2 = p.op("vector", lambda h: h.tensor_copy(cbb[:, :], pxb[:, :]),
                                 waits=[w_cb1])
                    w_cbbfree = w_cb2
                    w_my2 = p.dma("scalar", lambda h, ci=ci: h.dma_start(
                        out=myc2[:, :].rearrange("p (a t) -> p a t", a=8),
                        in_=my_r[:, :, ci * 512:(ci + 1) * 512]),
                        waits=[w_myfree] if w_myfree else [])
                    w_h = []
                    for fob in range(32):
                        bank = fob % 2
                        mm = None
                        for fi in range(8):
                            tw = []
                            if fi == 0:
                                tw = [w_my2, w_w1[7]]
                                if phev[bank] is not None:
                                    tw.append(phev[bank])
                            mm = p.op("tensor", lambda h, fi=fi, fob=fob, bank=bank:
                                      h.matmul(ph[bank][:, :],
                                               w1s[:, fi * 4096 + fob * 128:
                                                   fi * 4096 + (fob + 1) * 128],
                                               myc2[:, fi * 512:(fi + 1) * 512],
                                               start=(fi == 0), stop=(fi == 7)), waits=tw)
                        w_g = p.op("scalar", lambda h, ei=ei, fob=fob, bank=bank:
                                   h.activation(hbuf[:, fob * 512:(fob + 1) * 512],
                                                ph[bank][:, :], AF.Gelu,
                                                bias=b1s[:, ei * 32 + fob:ei * 32 + fob + 1]),
                                   waits=[mm, w_b1] + ([w_hfree] if (fob == 0 and w_hfree) else []))
                        phev[bank] = w_g
                        w_t = p.op("vector", lambda h, fob=fob: h.tensor_tensor(
                            hbuf[:, fob * 512:(fob + 1) * 512],
                            hbuf[:, fob * 512:(fob + 1) * 512],
                            cbb[:, :], ALU.mult), waits=[w_g, w_cb2])
                        w_h.append(w_t)
                    for fob2 in range(8):
                        bank = fob2 % 2
                        mm = None
                        for fi in range(32):
                            tw = []
                            if fi == 0:
                                tw = [w_h[-1], w_w2[31]]
                                if pyev[bank] is not None:
                                    tw.append(pyev[bank])
                            mm = p.op("tensor", lambda h, fi=fi, fob2=fob2, bank=bank:
                                      h.matmul(py[bank][:, :],
                                               w2s[:, fi * DM + fob2 * 128:
                                                   fi * DM + (fob2 + 1) * 128],
                                               hbuf[:, fi * 512:(fi + 1) * 512],
                                               start=(fi == 0), stop=False), waits=tw)
                        if ei == 0:
                            mm = p.op("tensor", lambda h, fob2=fob2, ci=ci, bank=bank:
                                      h.matmul(py[bank][:, :],
                                               b2s[:, fob2 * 128:(fob2 + 1) * 128],
                                               combfm[:, ci * 512:(ci + 1) * 512],
                                               start=False, stop=True),
                                      waits=[w_b2, w_comb_free])
                        else:
                            mm = p.op("tensor", lambda h, fob2=fob2, bank=bank:
                                      h.matmul(py[bank][:, :],
                                               w2s[:, 31 * DM + fob2 * 128:
                                                   31 * DM + (fob2 + 1) * 128],
                                               hbuf[:, 31 * 512:32 * 512],
                                               start=False, stop=True), waits=[])
                        yb = ysb[:, bank * 512:(bank + 1) * 512]
                        w_ye = p.op("vector", lambda h, yb=yb, bank=bank: h.tensor_copy(
                            yb, py[bank][:, :]),
                            waits=[mm] + ([w_ysbfree[bank]] if w_ysbfree[bank] else []))
                        pyev[bank] = w_ye
                        w_yd = p.dma("gpsimd", lambda h, ei=ei, fob2=fob2, c0=c0, yb=yb:
                                     h.dma_start(out=moe_d[fob2 * 128:(fob2 + 1) * 128,
                                                           c0:c0 + 512], in_=yb,
                                                 accum_op=(ALU.add if ei > 0 else ALU.bypass)),
                                     waits=[w_ye])
                        w_ysbfree[bank] = w_yd
                    w_myfree = w_h[-1] if False else w_h[-1]
                    w_hfree = mm
                w_w1free = phev[0] if phev[0] else None
                w_w1free = phev[1] if (phev[1] and phev[0] and phev[1][1] > phev[0][1]) else phev[0]
                w_w2free = pyev[1] if (pyev[1] and pyev[0] and pyev[1][1] > pyev[0][1]) else pyev[0]
            p.end_stage("L6M")

        with ExitStack() as ctx:
            e = ctx.enter_context
            mo = e(nc.sbuf_tensor("mo", [128, 8 * 512], F32))
            x1c2 = e(nc.sbuf_tensor("x1c2", [128, 8 * 512], F32))
            s1c2 = e(nc.sbuf_tensor("s1c2", [128, 8 * 512], F32))
            g1b = e(nc.sbuf_tensor("g1b", [128, 8 * 512], F32))
            otm = e(nc.sbuf_tensor("otm", [128, 4 * DM], F32))
            pt2 = [e(nc.psum_tensor(f"pt2{j}", [128, 128], F32)) for j in range(2)]
            w_mofree = None
            w_ofree = None
            ptev2 = [None, None]
            for ci in range(8):
                lw = [w_mofree] if w_mofree else []
                w_mo = p.dma("gpsimd", lambda h, ci=ci: h.dma_start(
                    out=mo[:, :].rearrange("p (a t) -> p a t", a=8),
                    in_=moe_r[:, :, ci * 512:(ci + 1) * 512]), waits=lw)
                w_x1 = p.dma("sync", lambda h, ci=ci: h.dma_start(
                    out=x1c2[:, :].rearrange("p (a t) -> p a t", a=8),
                    in_=x1_r[:, :, ci * 512:(ci + 1) * 512]), waits=lw)
                w_s1 = p.dma("scalar", lambda h, ci=ci: h.dma_start(
                    out=s1c2[:, :].rearrange("p (a t) -> p a t", a=8),
                    in_=s1_r[:, :, ci * 512:(ci + 1) * 512]), waits=lw)
                w_g1 = p.op("scalar", lambda h: h.activation(
                    g1b[:, :], mo[:, :], AF.Silu), waits=[w_mo] +
                    ([w_mofree] if w_mofree else []))
                w_p1 = p.op("vector", lambda h: h.tensor_tensor(
                    g1b[:, :], g1b[:, :], s1c2[:, :], ALU.mult), waits=[w_g1, w_s1])
                w_p2 = p.op("vector", lambda h: h.tensor_tensor(
                    g1b[:, :], g1b[:, :], x1c2[:, :], ALU.add), waits=[w_p1, w_x1])
                w_tp = []
                for ts4 in range(4):
                    for f in range(8):
                        bank = (ts4 * 8 + f) % 2
                        tw = [w_p2] + ([ptev2[bank]] if ptev2[bank] else [])
                        w1_ = p.op("tensor", lambda h, ts4=ts4, f=f, bank=bank: h.transpose(
                            pt2[bank][:, :],
                            g1b[:, f * 512 + ts4 * 128:f * 512 + (ts4 + 1) * 128],
                            C["identf"][:, :]), waits=tw)
                        w2_ = p.op("vector", lambda h, ts4=ts4, f=f, bank=bank: h.tensor_copy(
                            otm[:, ts4 * DM + f * 128:ts4 * DM + (f + 1) * 128],
                            pt2[bank][:, :]),
                            waits=[w1_] + ([w_ofree] if (ts4 == 0 and f == 0 and w_ofree) else []))
                        ptev2[bank] = w2_
                        w_tp.append(w2_)
                w_ofree = p.dma("gpsimd", lambda h, ci=ci: h.dma_start(
                    out=y_o.rearrange("(c p) d -> p c d", p=128)[:, 4 * ci:4 * ci + 4, :],
                    in_=otm[:, :].rearrange("p (c d) -> p c d", c=4)), waits=[w_tp[-1]])
                w_mofree = w_p2
            p.raw_wait("gpsimd", [(p.dma_sems["gpsimd"], p.dma_sems["gpsimd"].n)])
            p.end_stage("L6C")
    return nc


def l6_in_maps(ins, l4r, l5r):
    b1ch = np.ascontiguousarray(
        ins["L1_ffn_b1"].reshape(8, 32, 128).transpose(2, 0, 1).reshape(128, 256))
    bpch = np.ascontiguousarray(ins["L1_ffn_bp"].reshape(8, 128).T)
    maps = []
    for c in range(8):
        maps.append({
            "myxn": l4r[c]["xn1_o"],
            "x1": l4r[c]["x1_o"],
            "s1": l5r[c]["s1_o"],
            "wp": ins["L1_ffn_Wp"],
            "bpc": bpch,
            "cent": ins["L1_ffn_centroids"],
            "w1": ins["L1_ffn_W1"],
            "b1c": b1ch,
            "w2": ins["L1_ffn_W2"],
            "b2m": ins["L1_ffn_b2"],
        })
    return maps


from concourse.bass_utils import run_bass_kernel_spmd

_NC_CACHE = {}


def _get(name, builder):
    if name not in _NC_CACHE:
        _NC_CACHE[name] = builder()
    return _NC_CACHE[name]


def kernel(**ins):
    cores = list(range(8))
    nc1 = _get("L1", build_L1)
    r1 = run_bass_kernel_spmd(nc1, l1_in_maps(ins), core_ids=cores).results
    nc2 = _get("L2", build_L2)
    r2 = run_bass_kernel_spmd(nc2, l2_in_maps(ins, r1), core_ids=cores).results
    nc3 = _get("L3", build_L3)
    r3 = run_bass_kernel_spmd(nc3, l3_in_maps(ins, r1), core_ids=cores).results
    nc4 = _get("L4", build_L4)
    r4 = run_bass_kernel_spmd(nc4, l4_in_maps(ins, r1, r2, r3), core_ids=cores).results
    nc5 = _get("L5", build_L5)
    r5 = run_bass_kernel_spmd(nc5, l5_in_maps(ins, r4), core_ids=cores).results
    nc6 = _get("L6", build_L6)
    r6 = run_bass_kernel_spmd(nc6, l6_in_maps(ins, r4, r5), core_ids=cores).results
    out = np.empty((4, 8192, 1024), np.float32)
    for c in cores:
        b, hh = c // 2, c % 2
        out[b, hh * TH:(hh + 1) * TH] = r6[c]["y_o"]
    return out
